# version 36
# baseline (speedup 1.0000x reference)
import math
from collections import defaultdict
from contextlib import ExitStack

import numpy as np
import concourse.bass as bass
import concourse.mybir as mybir
from concourse.bass_utils import run_bass_kernel_spmd

F32 = mybir.dt.float32
BF16 = mybir.dt.bfloat16
AF = mybir.ActivationFunctionType
ALU = mybir.AluOpType
AX = mybir.AxisListType

NCORES = 8
T = 2048
NT = 16
D = 1024
DC = 8
EPS = 1e-5
INV_D = 1.0 / D
ISQ128 = 1.0 / math.sqrt(128.0)

ALL_PARTS = (0, 1, 2, 3)


_V_OPS = {
    "tensor_copy", "tensor_tensor", "tensor_add", "tensor_sub", "tensor_mul",
    "tensor_max", "tensor_scalar", "tensor_single_scalar", "tensor_scalar_add",
    "tensor_scalar_sub", "tensor_scalar_mul", "tensor_scalar_max", "tensor_scalar_min",
    "tensor_relu", "tensor_reduce", "reduce_max", "reduce_sum", "scalar_tensor_tensor",
    "tensor_tensor_scan", "memset", "memzero", "reciprocal", "select", "transpose",
    "copy_predicated",
}


_DT_SIZE = {mybir.dt.float32: 4, mybir.dt.bfloat16: 2, mybir.dt.float16: 2,
            mybir.dt.int32: 4, mybir.dt.int8: 1}


def _ap_rng(a):
    try:
        name = a.tensor.name
        sz = _DT_SIZE.get(a.dtype, 4)
        span = 0
        for step, count in list(a.ap)[1:]:
            span += int(step) * (int(count) - 1)
        return (name, int(a.offset) * sz, (int(a.offset) + span + 1) * sz)
    except Exception:
        try:
            return (a.tensor.name, 0, 1 << 60)
        except Exception:
            return None


def _rngs_overlap(r1, r2):
    for a in r1:
        for b in r2:
            if a and b and a[0] == b[0] and a[1] < b[2] and b[1] < a[2]:
                return True
    return False


class _VSync:

    def __init__(self, e, sem, cnt, apcls, state):
        self._e = e
        self._sem = sem
        self._cnt = cnt
        self._ap = apcls
        self._st = state

    def note_sync(self, ins, semh, val):
        if self._st.get("last") is ins:
            self._st["sync"] = (semh, val)

    def __getattr__(self, name):
        fn = getattr(self._e, name)
        if name not in _V_OPS:
            return fn

        def wrapped(*args, **kw):
            aps = [a for a in args if isinstance(a, self._ap)]
            outs = [_ap_rng(aps[0])] if aps else []
            ins_n = [_ap_rng(a) for a in aps[1:]]
            if "accum_out" in kw and isinstance(kw["accum_out"], self._ap):
                outs.append(_ap_rng(kw["accum_out"]))
            st = self._st
            if _rngs_overlap(ins_n, st["w"]) or _rngs_overlap(outs, st["r"] + st["w"]):
                if st.get("last") is not None:
                    if st.get("sync") is None:
                        st["last"].then_inc(self._sem, 1)
                        self._cnt[0] += 1
                        st["sync"] = (self._sem, self._cnt[0])
                    self._e.wait_ge(*st["sync"])
            r = fn(*args, **kw)
            st["r"], st["w"] = ins_n, outs
            st["last"], st["sync"] = r, None
            return r

        return wrapped


def build_program(parts=ALL_PARTS, taps=()):
    nc = bass.Bass("TRN2", target_bir_lowering=False, detect_race_conditions=False)
    es = ExitStack()
    S = {}
    cnt = defaultdict(int)
    mark = {}
    snap = defaultdict(int)

    vv_cnt = [0]
    vv_state = {"r": [], "w": [], "last": None, "sync": None}

    def wrapv(e):
        return _VSync(e, sem("vv"), vv_cnt, bass.AP, vv_state)

    def sem(name):
        if name not in S:
            S[name] = es.enter_context(nc.semaphore(name))
        return S[name]

    def inc(ins, name, k=1):
        ins.then_inc(sem(name), k)
        cnt[name] += k
        if vv_state.get("last") is ins:
            vv_state["sync"] = (sem(name), cnt[name])
        return ins

    def wait(e, name, v):
        if v > 0:
            e.wait_ge(sem(name), v)

    def set_mark(name):
        mark[name] = dict(cnt)

    def wait_mark(e, name, sems):
        if name in mark:
            for s in sems:
                wait(e, s, mark[name].get(s, 0))

    def take_snap():
        snap.clear()
        snap.update(cnt)

    _BARRIER_SKIP = {"wl", "st", "lx"}

    def barrier(e, skip=()):
        for name, v in list(snap.items()):
            if name not in _BARRIER_SKIP and name not in skip:
                wait(e, name, v)

    dr = {}
    for name, shape in [
        ("x_in", (T, D)),
        ("l0wq", (D, D)), ("l0wk", (D, D)), ("l0wv", (D, D)),
        ("l0bq", (128, 8)), ("l0bk", (128, 8)), ("l0bv", (1, D)),
        ("l0bw", (D, D)), ("l0bb", (1, D)),
        ("l0sm", (D, 16)), ("l0rb", (1, 16)), ("l0lb", (64, D)),
        ("l1wq", (D, D)), ("l1wk", (D, D)), ("l1wv", (D, D)),
        ("l1rw", (D, 128)), ("l1ce", (16, 128)), ("l1tm", (1, 1)),
        ("l1we", (16, D, 512)), ("l1wo", (16, 512, 64)),
    ]:
        dr[name] = nc.dram_tensor(name, list(shape), F32, kind="ExternalInput")
    out = nc.dram_tensor("out", [T, D], F32, kind="ExternalOutput")

    sb = lambda n, sh, dt: es.enter_context(nc.sbuf_tensor(n, list(sh), dt))
    psb = lambda n, sh, dt: es.enter_context(nc.psum_tensor(n, list(sh), dt))

    xs = sb("xs", (128, NT, D), F32)
    xbf = sb("xbf", (128, DC, T), BF16)
    kbuf = sb("kbuf", (128, DC, T), BF16)
    vbuf = sb("vbuf", (128, NT, D), BF16)
    w2 = sb("w2", (128, 2, 8192), BF16)
    u8 = sb("u8", (128, 4096), BF16)
    m640 = sb("m640", (128, 1664), BF16)
    xbt = sb("xbt", (128, 1, D), BF16)
    stats = sb("stats", (128, NT, 8), F32)
    cst = sb("cst", (128, 2), F32)
    macc = sb("macc", (128, NT), F32)
    s1x = sb("s1x", (128, NT), F32)
    ident = sb("ident", (128, 128), BF16)
    ones_row = sb("ones_row", (1, 128), BF16)
    onescol = sb("onescol", (128, 1), BF16)
    bqc = sb("bqc", (128, 8), BF16)
    bkc = sb("bkc", (128, 8), BF16)
    wsm = sb("wsm", (128, DC, 16), BF16)
    rb_row = sb("rb_row", (1, 16), BF16)
    gsc = sb("gsc", (128, 8), F32)
    g_bf = sb("g_bf", (128, 8), BF16)
    g2f = sb("g2f", (128, 8), F32)
    cent = sb("cent", (16, 128), F32)
    csc = sb("csc", (16, 2), F32)
    cnT = sb("cnT", (128, 16), BF16)
    tmp_r = sb("tmp_r", (1, 1), F32)
    et_bf = sb("et_bf", (1, 1), BF16)
    etb = sb("etb", (128, 1), F32)
    lsc = sb("lsc", (128, 16), F32)
    s16a = sb("s16a", (128, 16), F32)
    s16b = sb("s16b", (128, 16), F32)

    wlb = w2[0:64, 0, 0:1024]
    t64T = w2[0:64, 0, 1024:1152]
    t64 = w2[:, 0, 1152:1216]

    pA = psb("pA", (128, 2048), F32)
    pB = psb("pB", (128, 1024), F32)
    pT = psb("pT", (128, 2, 1024), BF16)

    probs = u8[:, 0:2048]
    attnT = lambda i: u8[:, 2048 + i * 128: 2048 + (i + 1) * 128]
    z_bf = u8[:, 0:2048]
    z2_bf = u8[:, 2048:4096]
    cn_bf = m640[0:16, 256:384]

    def we_ap(n, lo, hi):
        if n % 3 < 2:
            return w2[:, 0, (n % 3) * 4096 + lo:(n % 3) * 4096 + hi]
        return u8[:, lo:hi]

    def we_dst(n):
        if n % 3 < 2:
            return w2[:, 0, (n % 3) * 4096:(n % 3) * 4096 + 4096].rearrange(
                "p (dc h) -> p dc h", dc=8)
        return u8[:, 0:4096].rearrange("p (dc h) -> p dc h", dc=8)
    brow = m640[0:1, 640:1664]
    mask_rel = m640[:, 0:640]
    wout = m640[:, 0:256].rearrange("p (hc o) -> p hc o", hc=4)
    w_all = m640[:, 256:768].bitcast(F32).rearrange("p (t n) -> p t n", t=NT)
    rv = vbuf[:, 14, :].bitcast(F32)
    rv2 = vbuf[:, 15, :].bitcast(F32)
    lsc_all = rv[:, 0:256]
    s16all = rv[:, 256:512]
    exm = rv2[:, 0:256]
    rstdA = rv2[:, 256:272]
    rstd16 = rv2[:, 272:288]
    rscl16 = rv2[:, 288:304]
    rsum16 = rv2[:, 304:320]
    m1_16 = rv2[:, 320:336]

    def z_slot(t, lo, hi):
        off = (t % 2) * 1024
        return kbuf[:, t // 2, off + lo: off + hi]

    def wslot(i):
        return w2[:, i, :]


    def emit_setup(blk):
        take_snap()

        @blk.sync
        def _(e):
            inc(e.dma_start(cent[:, :], dr["l1ce"][:, :]), "ld", 16)
            inc(e.dma_start(tmp_r[:, :], dr["l1tm"][:, :]), "ld", 16)
            x_src = dr["x_in"].rearrange("(t p) d -> p t d", p=128)
            for c in range(8):
                inc(e.dma_start(xs[:, c * 2:(c + 1) * 2, :], x_src[:, c * 2:(c + 1) * 2, :]),
                    "lx", 16)

        @blk.gpsimd
        def _(e):
            inc(e.dma_start(bqc[:, :], dr["l0bq"][:, :]), "wl", 16)
            inc(e.dma_start(bkc[:, :], dr["l0bk"][:, :]), "wl", 16)
            inc(e.dma_start(w2[:, 0, :].rearrange("p (dc o) -> p dc o", dc=8), dr["l0wk"].rearrange("(dc p) o -> p dc o", p=128)), "wl", 16)
            set_mark("wk0")
            inc(e.dma_start(w2[:, 1, :].rearrange("p (dc o) -> p dc o", dc=8), dr["l0wv"].rearrange("(dc p) o -> p dc o", p=128)), "wl", 16)
            set_mark("wv0")
            inc(e.dma_start(brow, dr["l0bv"][:, :]), "wl", 16)
            set_mark("bv")
            inc(e.dma_start(wsm[:, :, :], dr["l0sm"].rearrange("(dc p) o -> p dc o", p=128)), "wl", 16)
            inc(e.dma_start(rb_row[:, :], dr["l0rb"][:, :]), "wl", 16)
            set_mark("smalls")
            e.wait_ge(sem("dv"), 1)
            inc(e.affine_select(ident[:, :], ident[:, :], [[1, 128]], ALU.is_equal, 0.0,
                                base=0, channel_multiplier=-1), "gp", 1)

        @blk.vector
        def _(e):
            e = wrapv(e)
            e.memset(ones_row[:, :], 1.0)
            e.memset(onescol[:, :], 1.0)
            e.memset(cst[:, 0:1], EPS)
            e.memset(cst[:, 1:2], 1e-24)
            inc(e.memset(ident[:, :], 1.0), "dv", 1)

    def make_preln_hooks():
        base = {"hvr": cnt["hvr"], "hrr": cnt["hrr"]}

        def dve_stats(e, j):
            e.scalar_tensor_tensor(xbt[:, 0, :], xs[:, j, :], 1.0, xs[:, j, :],
                                   ALU.mult, ALU.mult, accum_out=stats[:, j, 1:2])
            e.tensor_scalar_mul(stats[:, j, 2:3], s1x[:, j:j + 1], INV_D)
            e.tensor_mul(stats[:, j, 7:8], stats[:, j, 2:3], stats[:, j, 2:3])
            inc(e.tensor_scalar(stats[:, j, 3:4], stats[:, j, 1:2], INV_D,
                                stats[:, j, 7:8], ALU.mult, ALU.subtract), "hvr", 1)

        def act_stats(e, j):
            wait(e, "hvr", base["hvr"] + j + 1)
            e.activation(stats[:, j, 6:7], stats[:, j, 3:4], AF.Ln, bias=cst[:, 0:1])
            inc(e.activation(stats[:, j, 4:5], stats[:, j, 6:7], AF.Exp, scale=-0.5), "hrr", 1)

        return {"dve": dve_stats, "act": act_stats, "base": base}

    def make_preln_hooks_v2(do_apply=True):
        base = {"hsq": cnt["hsq"], "hvr": cnt["hvr"], "hrr": cnt["hrr"],
                "hap": cnt["hap"], "pl": cnt["pl"]}
        p_pl = base["pl"]

        def sq(e, k):
            wait(e, "pl", p_pl + k + 1)
            inc(e.activation(xbt[:, 0, :], xs[:, k, :], AF.Square,
                             accum_out=stats[:, k, 1:2]), "hsq", 1)

        def lnexp(e, k):
            wait(e, "hvr", base["hvr"] + k + 1)
            e.activation(stats[:, k, 6:7], stats[:, k, 3:4], AF.Ln, bias=cst[:, 0:1])
            inc(e.activation(stats[:, k, 4:5], stats[:, k, 6:7], AF.Exp, scale=-0.5), "hrr", 1)

        def var(e, k):
            wait(e, "hsq", base["hsq"] + k + 1)
            inc(e.tensor_scalar(stats[:, k, 3:4], stats[:, k, 1:2], INV_D,
                                stats[:, k, 7:8], ALU.mult, ALU.subtract), "hvr", 1)

        def apply(e, k):
            wait(e, "hrr", base["hrr"] + k + 1)
            inc(e.tensor_scalar(vbuf[:, k, :], xs[:, k, :],
                                stats[:, k, 2:3], stats[:, k, 4:5],
                                ALU.subtract, ALU.mult), "hap", 1)

        def act_hook(e, j):
            if j >= 1:
                sq(e, j - 1)
            if j >= 3:
                lnexp(e, j - 3)

        def dve_hook(e, j):
            e.tensor_scalar_mul(stats[:, j, 2:3], s1x[:, j:j + 1], INV_D)
            if j >= 1:
                e.tensor_mul(stats[:, j - 1, 7:8], stats[:, j - 1, 2:3], stats[:, j - 1, 2:3])
            if do_apply and j >= 4:
                apply(e, j - 4)
            if j >= 2:
                var(e, j - 2)

        def tail_act(e):
            sq(e, NT - 1)
            for k in range(NT - 3, NT):
                lnexp(e, k)

        def tail_vec(e):
            e.tensor_mul(stats[:, NT - 1, 7:8], stats[:, NT - 1, 2:3], stats[:, NT - 1, 2:3])
            for k in range(NT - 2, NT):
                var(e, k)
            if do_apply:
                for k in range(NT - 4, NT):
                    apply(e, k)

        return {"v2": True, "do_apply": do_apply, "dve": dve_hook, "act": act_hook,
                "base": base, "tail_act": tail_act, "tail_vec": tail_vec}

    def emit_preln(blk, prev=None, hooks=None):
        take_snap()
        v2 = hooks is not None and hooks.get("v2")
        if hooks is not None and not v2:
            b_hrr = hooks["base"]["hrr"]
        b_sq, b_vr, b_rr = snap["sq"], snap["vr"], snap["rr"]
        b_ap, b_tr, b_ev = snap["ap"], snap["tr"], snap["ev"]
        p_pl = snap["pl"] - NT
        p_oo = snap["oo"] - NT
        p_ze = snap["ze"] - NT

        if hooks is None:
            @blk.scalar
            def _(e):
                if prev is None:
                    barrier(e)
                for t in range(NT):
                    if prev is not None:
                        wait(e, "pl", p_pl + t + 1)
                    else:
                        wait(e, "lx", (t // 2 + 1) * 16)
                    inc(e.activation(xbt[:, 0, :], xs[:, t, :], AF.Square,
                                     accum_out=stats[:, t, 1:2]), "sq", 1)
                    wait(e, "vr", b_vr + t + 1)
                    e.activation(stats[:, t, 6:7], stats[:, t, 3:4], AF.Ln, bias=cst[:, 0:1])
                    inc(e.activation(stats[:, t, 4:5], stats[:, t, 6:7], AF.Exp, scale=-0.5), "rr", 1)

        @blk.vector
        def _(e):
            e = wrapv(e)
            if prev is None:
                barrier(e)
            if v2:
                hooks["tail_vec"](e)
                if not hooks["do_apply"]:
                    if prev == "l1a":
                        wait(e, "oo", p_oo + NT)
                    for t in range(NT):
                        wait(e, "hrr", hooks["base"]["hrr"] + t + 1)
                        inc(e.tensor_scalar(vbuf[:, t, :], xs[:, t, :],
                                            stats[:, t, 2:3], stats[:, t, 4:5],
                                            ALU.subtract, ALU.mult), "ap", 1)
                return
            if hooks is None:
                for t in range(NT):
                    if prev is None:
                        wait(e, "lx", (t // 2 + 1) * 16)
                    e.reduce_sum(s1x[:, t:t + 1], xs[:, t, :], axis=AX.X)
                    e.tensor_scalar_mul(stats[:, t, 2:3], s1x[:, t:t + 1], INV_D)
                    e.tensor_mul(stats[:, t, 7:8], stats[:, t, 2:3], stats[:, t, 2:3])
                    wait(e, "sq", b_sq + t + 1)
                    inc(e.tensor_scalar(stats[:, t, 3:4], stats[:, t, 1:2], INV_D,
                                        stats[:, t, 7:8], ALU.mult, ALU.subtract), "vr", 1)
            if prev == "l1a":
                wait(e, "oo", p_oo + NT)
            for t in range(NT):
                if prev in ("l0a", "ffn0"):
                    wait(e, "oo", p_oo + t + 1)
                    wait(e, "ze", p_ze + t + 1)
                wait(e, "rr" if hooks is None else "hrr",
                     (b_rr if hooks is None else b_hrr) + t + 1)
                inc(e.tensor_scalar(vbuf[:, t, :], xs[:, t, :],
                                    stats[:, t, 2:3], stats[:, t, 4:5],
                                    ALU.subtract, ALU.mult), "ap", 1)

        @blk.tensor
        def _(e):
            if prev is None:
                barrier(e)
            else:
                wait(e, "ev", snap["ev"])
                wait(e, "at", snap["at"])
                wait(e, "te", snap["te"])
                wait(e, "pp", snap["pp"])
            for t in range(NT):
                if v2 and hooks["do_apply"]:
                    wait(e, "hap", hooks["base"]["hap"] + t + 1)
                else:
                    wait(e, "ap", b_ap + t + 1)
                if t >= 2:
                    wait(e, "ev", b_ev + t - 1)
                ins = None
                for dc in range(DC):
                    ins = e.transpose(pT[:, t % 2, dc * 128:(dc + 1) * 128],
                                      vbuf[:, t, dc * 128:(dc + 1) * 128],
                                      ident[:, :])
                inc(ins, "tr", 1)

        @blk.scalar
        def _(e):
            for t in range(NT):
                if v2 and t == NT - 4:
                    hooks["tail_act"](e)
                wait(e, "tr", b_tr + t + 1)
                src_ = pT[:, t % 2, :].rearrange("p (dc q) -> p dc q", dc=DC)
                inc(e.activation(xbf[:, :, t * 128:(t + 1) * 128], src_, AF.Copy), "ev", 1)

    def emit_proj_kvq(blk, layer):
        l0 = (layer == 0)
        take_snap()
        b_kg, b_kev = snap["kg"], snap["kev"]
        b_vg, b_vev = snap["vg"], snap["vev"]
        b_qg, b_qev = snap["qg"], snap["qev"]

        b_evp = snap["ev"] - NT

        @blk.tensor
        def _(e):
            barrier(e, skip=("ev",))
            wait_mark(e, "wk0" if l0 else "wk1", ["wl"])
            for ec in range(DC):
                for g in range(4):
                    wait(e, "ev", b_evp + 4 * (g + 1))
                    if ec > 0:
                        wait(e, "kev", b_kev + (ec - 1) * 4 + g + 1)
                    ins = None
                    for dc in range(DC):
                        ins = e.matmul(pA[:, g * 512:(g + 1) * 512],
                                       w2[:, 0, dc * 1024 + ec * 128: dc * 1024 + (ec + 1) * 128],
                                       xbf[:, dc, g * 512:(g + 1) * 512],
                                       start=(dc == 0), stop=(dc == DC - 1))
                    inc(ins, "kg", 1)
            set_mark(f"l{layer}k")

        @blk.scalar
        def _(e):
            for ec in range(DC):
                for g in range(4):
                    wait(e, "kg", b_kg + ec * 4 + g + 1)
                    if l0:
                        ins = e.activation(kbuf[:, ec, g * 512:(g + 1) * 512],
                                           pA[:, g * 512:(g + 1) * 512], AF.Identity,
                                           bias=bkc[:, ec:ec + 1])
                    else:
                        ins = e.activation(kbuf[:, ec, g * 512:(g + 1) * 512],
                                           pA[:, g * 512:(g + 1) * 512], AF.Copy)
                    inc(ins, "kev", 1)

        @blk.gpsimd
        def _(e):
            wait_mark(e, f"l{layer}k", ["kg"])
            wq = dr["l0wq" if l0 else "l1wq"]
            inc(e.dma_start(w2[:, 0, :].rearrange("p (dc o) -> p dc o", dc=8), wq.rearrange("(dc p) o -> p dc o", p=128)), "wl", 16)
            set_mark(f"wq{layer}")

        @blk.tensor
        def _(e):
            wait_mark(e, "wv0" if l0 else "wv1", ["wl"])
            if l0:
                wait_mark(e, "bv", ["wl"])
            wait(e, "kev", b_kev + 32)
            for t in range(NT):
                wait(e, "ev", b_evp + t + 1)
                if t >= 2:
                    wait(e, "vev", b_vev + t - 1)
                po = (t % 2) * 1024
                ins = None
                for og in range(2):
                    for dc in range(DC):
                        ins = e.matmul(pA[:, po + og * 512: po + (og + 1) * 512],
                                       xbf[:, dc, t * 128:(t + 1) * 128],
                                       w2[:, 1, dc * 1024 + og * 512: dc * 1024 + (og + 1) * 512],
                                       start=(dc == 0), stop=(dc == DC - 1 and not l0))
                    if l0:
                        ins = e.matmul(pA[:, po + og * 512: po + (og + 1) * 512],
                                       ones_row[:, :], brow[:, og * 512:(og + 1) * 512],
                                       start=False, stop=True)
                inc(ins, "vg", 1)
            set_mark(f"l{layer}v")

        @blk.scalar
        def _(e):
            for t in range(NT):
                wait(e, "vg", b_vg + t + 1)
                po = (t % 2) * 1024
                inc(e.activation(vbuf[:, t, :], pA[:, po:po + 1024], AF.Copy), "vev", 1)

        @blk.gpsimd
        def _(e):
            wait_mark(e, f"l{layer}v", ["vg"])
            if l0:
                inc(e.dma_start(brow, dr["l0bb"][:, :]), "wl", 16)
                set_mark("bb")
                inc(e.dma_start(w2[:, 1, :].rearrange("p (dc o) -> p dc o", dc=8), dr["l0bw"].rearrange("(dc p) o -> p dc o", p=128)), "wl", 16)
                set_mark("bw")
            else:
                inc(e.dma_start(w2[:, 1, 0:1024].rearrange("p (dc o) -> p dc o", dc=8), dr["l1rw"].rearrange("(dc p) o -> p dc o", p=128)), "wl", 16)
                set_mark("wrw")

        @blk.tensor
        def _(e):
            wait_mark(e, f"wq{layer}", ["wl"])
            wait(e, "vev", b_vev + NT)
            for g8 in range(8):
                if g8 >= 1:
                    wait(e, "qev", b_qev + g8)
                ins = None
                for ec in range(DC):
                    for dc in range(DC):
                        ins = e.matmul(pA[:, ec * 256:(ec + 1) * 256],
                                       w2[:, 0, dc * 1024 + ec * 128: dc * 1024 + (ec + 1) * 128],
                                       xbf[:, dc, g8 * 256:(g8 + 1) * 256],
                                       start=(dc == 0), stop=(dc == DC - 1))
                inc(ins, "qg", 1)
            set_mark(f"l{layer}q")

        @blk.scalar
        def _(e):
            for g8 in range(8):
                wait(e, "qg", b_qg + g8 + 1)
                ins = None
                for ec in range(DC):
                    if l0:
                        ins = e.activation(xbf[:, ec, g8 * 256:(g8 + 1) * 256],
                                           pA[:, ec * 256:(ec + 1) * 256], AF.Identity,
                                           bias=bqc[:, ec:ec + 1])
                    else:
                        ins = e.activation(xbf[:, ec, g8 * 256:(g8 + 1) * 256],
                                           pA[:, ec * 256:(ec + 1) * 256], AF.Copy,
                                           scale=1.0 / 32.0)
                inc(ins, "qev", 1)

        @blk.gpsimd
        def _(e):
            if l0:
                wait_mark(e, "l0q", ["qg"])
                inc(e.dma_start(wlb, dr["l0lb"][:, :]), "wl", 16)
                set_mark("wlb")
            if not l0:
                wait_mark(e, "l1q", ["qg"])
                for n in range(2):
                    inc(e.dma_start(w2[:, 0, n * 4096:(n + 1) * 4096].rearrange("p (dc h) -> p dc h", dc=8),
                                    dr["l1we"][n].rearrange("(dc p) h -> p dc h", p=128)), "wl", 16)
                    set_mark(f"we{n}")

    def emit_build_mask(blk):
        take_snap()

        @blk.vector
        def _(e):
            e = wrapv(e)
            barrier(e)
            e.memset(probs, 0.0)
            inc(e.memset(mask_rel, 1.0), "dv", 1)

        @blk.gpsimd
        def _(e):
            wait_mark(e, "l0f_mm", ["oo"])
            wait(e, "dv", cnt["dv"])
            e.affine_select(mask_rel, mask_rel, [[1, 640]], ALU.is_ge, 0.0,
                            base=0, channel_multiplier=-1)
            inc(e.affine_select(mask_rel, mask_rel, [[-1, 640]], ALU.is_ge, 0.0,
                                base=511, channel_multiplier=1), "gp", 1)

    def emit_attn_core(blk, layer, trunc=6, first_attn=True, hooks=None):
        l0 = (layer == 0)
        nk = 1 if l0 else NT
        scw = 128 if l0 else 2048
        take_snap()
        b_sc, b_mx, b_ex = snap["sc"], snap["mx"], snap["ex"]
        b_tr, b_at, b_oo = snap["tr"], snap["at"], snap["oo"]
        b_pl, b_s2, b_vr2, b_rr = snap["pl"], snap["s2p"], snap["vr2"], snap["rr"]
        b_ze, b_fl = snap["ze"], snap["fl"]

        sc_ap = lambda t: pA[:, (t % 8) * 128:(t % 8) * 128 + 128]
        out_ps = (lambda j: pB[:, 0:1024] if j % 2 == 0 else pA[:, 1024:2048]) if l0 \
            else (lambda j: pB[:, 0:1024])
        pr_ap = (lambda t: probs[:, t * 128:(t + 1) * 128]) if l0 \
            else (lambda t: probs[:, 0:scw])
        atT = (lambda t, i: attnT(t)) if l0 else (lambda t, i: attnT(i))

        def emit_out(e, j):
            if l0:
                if j >= 2:
                    wait(e, "ze", b_ze + j - 1)
            else:
                wait(e, "pl", b_pl + j)
            bp = out_ps(j)
            ins = None
            for og in range(2):
                for i in range(nk):
                    kt = j if l0 else i
                    if og == 0:
                        wait(e, "at", b_at + j * nk + i + 1)
                    ins = e.matmul(bp[:, og * 512:(og + 1) * 512],
                                   atT(j, i),
                                   vbuf[:, kt, og * 512:(og + 1) * 512],
                                   start=(i == 0), stop=(i == nk - 1))
            inc(ins, "oo", 1)

        @blk.tensor
        def _(e):
            barrier(e)
            n_tr = 0
            if l0:
                for g in range(4):
                    if g >= 2:
                        wait(e, "ex", b_ex + (g - 1) * 4)
                    for t in range(g * 4, g * 4 + 4):
                        ins = None
                        for ec in range(DC):
                            ins = e.matmul(sc_ap(t),
                                           xbf[:, ec, t * 128:(t + 1) * 128],
                                           kbuf[:, ec, t * 128:(t + 1) * 128],
                                           start=(ec == 0), stop=(ec == DC - 1))
                        inc(ins, "sc", 1)
                for t in range(NT):
                    if t >= 1:
                        emit_out(e, t - 1)
                    wait(e, "ex", b_ex + t + 1)
                    if n_tr >= 2:
                        wait(e, "at", b_at + n_tr - 1)
                    inc(e.transpose(pT[:, n_tr % 2, 0:128],
                                    pr_ap(t), ident[:, :]), "tr", 1)
                    n_tr += 1
                emit_out(e, NT - 1)
                set_mark(f"l{layer}att")
                return
            for t in range(NT):
                if t >= 1:
                    wait(e, "ex", b_ex + t)
                lo = max(0, 128 * t - 256)
                hi = min(2048, 128 * t + 384)
                ins = None
                for kg in range(lo // 512, (hi + 511) // 512):
                    for ec in range(DC):
                        ins = e.matmul(pA[:, kg * 512:(kg + 1) * 512],
                                       xbf[:, ec, t * 128:(t + 1) * 128],
                                       kbuf[:, ec, kg * 512:(kg + 1) * 512],
                                       start=(ec == 0), stop=(ec == DC - 1))
                inc(ins, "sc", 1)
                if t >= 1:
                    emit_out(e, t - 1)
                wait(e, "fl", b_fl + t + 1)
                for i in range(nk):
                    if n_tr >= 2:
                        wait(e, "at", b_at + n_tr - 1)
                    ins = e.transpose(pT[:, n_tr % 2, 0:128],
                                      probs[:, i * 128:(i + 1) * 128],
                                      ident[:, :])
                    inc(ins, "tr", 1)
                    n_tr += 1
            emit_out(e, NT - 1)
            set_mark(f"l{layer}att")

        @blk.vector
        def _(e):
            e = wrapv(e)
            barrier(e)
            n_tr = 0

            def pl_front(j):
                bp = vbuf[:, j, :] if l0 else out_ps(j)
                wait(e, "oo", b_oo + j + 1)
                wait(e, "s2p", b_s2 + j + 1)
                if l0:
                    wait(e, "ze", b_ze + j + 1)
                rinv = stats[:, j, 7:8]
                e.tensor_scalar(stats[:, j, 2:3], stats[:, j, 0:1], rinv, INV_D,
                                ALU.mult, ALU.mult)
                e.tensor_scalar(stats[:, j, 3:4], stats[:, j, 1:2], rinv, rinv,
                                ALU.mult, ALU.mult)
                e.tensor_mul(stats[:, j, 1:2], stats[:, j, 2:3], stats[:, j, 2:3])
                inc(e.tensor_scalar(stats[:, j, 3:4], stats[:, j, 3:4], INV_D,
                                    stats[:, j, 1:2], ALU.mult, ALU.subtract), "vr2", 1)

            def pl_back(j):
                bp = vbuf[:, j, :] if l0 else out_ps(j)
                rinv = stats[:, j, 7:8]
                wait(e, "rr", b_rr + j + 1)
                e.tensor_mul(stats[:, j, 5:6], stats[:, j, 2:3], stats[:, j, 4:5])
                e.tensor_mul(stats[:, j, 1:2], stats[:, j, 4:5], rinv)
                if layer == 0 and first_attn:
                    e.tensor_copy(macc[:, j:j + 1], stats[:, j, 5:6])
                else:
                    e.tensor_add(macc[:, j:j + 1], macc[:, j:j + 1], stats[:, j, 5:6])
                e.scalar_tensor_tensor(s1x[:, j:j + 1], stats[:, j, 5:6], float(D),
                                       s1x[:, j:j + 1], ALU.mult, ALU.add)
                inc(e.scalar_tensor_tensor(xs[:, j, :], bp, stats[:, j, 1:2],
                                           xs[:, j, :], ALU.mult, ALU.add), "pl", 1)
                if hooks is not None:
                    hooks["dve"](e, j)

            def postln(j):
                pl_front(j)
                pl_back(j)

            if l0:
                for g in range(4):
                    wait(e, "sc", b_sc + (g + 1) * 4)
                    if g >= 2:
                        wait(e, "ex", b_ex + (g - 1) * 4)
                    for t in range(g * 4, g * 4 + 4):
                        inc(e.reduce_max(stats[:, t, 6:7], sc_ap(t), axis=AX.X,
                                         negate=True), "mx", 1)
                for t in range(NT):
                    wait(e, "ex", b_ex + t + 1)
                    e.reduce_sum(stats[:, t, 6:7], pr_ap(t), axis=AX.X)
                    e.reciprocal(stats[:, t, 7:8], stats[:, t, 6:7])
                    if t >= 1:
                        pl_front(t - 1)
                    if t >= 2:
                        pl_back(t - 2)
                pl_front(NT - 1)
                pl_back(NT - 2)
                pl_back(NT - 1)
            else:
                for t in range(NT):
                    wait(e, "sc", b_sc + t + 1)
                    lo = max(0, 128 * t - 256)
                    hi = min(2048, 128 * t + 384)
                    off = lo - (128 * t - 256)
                    e.tensor_tensor(pA[:, lo:hi], pA[:, lo:hi],
                                    mask_rel[:, off:off + (hi - lo)], ALU.mult)
                    e.reduce_max(stats[:, t, 6:7], pA[:, lo:hi], axis=AX.X, negate=True)
                    inc(e.tensor_single_scalar(stats[:, t, 6:7], stats[:, t, 6:7],
                                               0.0, ALU.min), "mx", 1)
                    wait(e, "ex", b_ex + t + 1)
                    ins = None
                    if lo > 0:
                        ins = e.tensor_scalar(probs[:, 0:lo], probs[:, 0:lo], 0.0,
                                              stats[:, t, 5:6], ALU.mult, ALU.add)
                    if hi < 2048:
                        ins = e.tensor_scalar(probs[:, hi:2048], probs[:, hi:2048], 0.0,
                                              stats[:, t, 5:6], ALU.mult, ALU.add)
                    inc(ins, "fl", 1)
                    e.reduce_sum(stats[:, t, 0:1], probs[:, lo:hi], axis=AX.X)
                    e.scalar_tensor_tensor(stats[:, t, 0:1], stats[:, t, 5:6],
                                           float(2048 - (hi - lo)),
                                           stats[:, t, 0:1], ALU.mult, ALU.add)
                    e.reciprocal(stats[:, t, 7:8], stats[:, t, 0:1])
                    if t >= 1:
                        postln(t - 1)
                postln(NT - 1)

        @blk.scalar
        def _(e):
            def postact(j):
                wait(e, "oo", b_oo + j + 1)
                if l0:
                    inc(e.activation(vbuf[:, j, :], out_ps(j), AF.Identity,
                                     accum_out=stats[:, j, 0:1]), "ze", 1)
                    inc(e.activation(xbt[:, 0, :], vbuf[:, j, :], AF.Square,
                                     accum_out=stats[:, j, 1:2]), "s2p", 1)
                else:
                    e.activation(xbt[:, 0, :], out_ps(j), AF.Identity,
                                 accum_out=stats[:, j, 0:1])
                    inc(e.activation(xbt[:, 0, :], out_ps(j), AF.Square,
                                     accum_out=stats[:, j, 1:2]), "s2p", 1)
                wait(e, "vr2", b_vr2 + j + 1)
                e.activation(stats[:, j, 6:7], stats[:, j, 3:4], AF.Ln, bias=cst[:, 0:1])
                inc(e.activation(stats[:, j, 4:5], stats[:, j, 6:7], AF.Exp, scale=-0.5), "rr", 1)
                if hooks is not None:
                    hooks["act"](e, j)

            n_tr = 0
            if l0:
                for g in range(4):
                    wait(e, "mx", b_mx + (g + 1) * 4)
                    for t in range(g * 4, g * 4 + 4):
                        inc(e.activation(pr_ap(t), sc_ap(t), AF.Exp,
                                         bias=stats[:, t, 6:7]), "ex", 1)
                    if g >= 1:
                        for j in range((g - 1) * 4, g * 4):
                            wait(e, "tr", b_tr + j + 1)
                            inc(e.activation(atT(j, 0), pT[:, j % 2, 0:128], AF.Copy), "at", 1)
                            if j >= 2:
                                postact(j - 2)
                for j in range(12, 16):
                    wait(e, "tr", b_tr + j + 1)
                    inc(e.activation(atT(j, 0), pT[:, j % 2, 0:128], AF.Copy), "at", 1)
                    postact(j - 2)
                postact(14)
                postact(15)
            else:
                for t in range(NT + 1):
                    if t < NT:
                        lo = max(0, 128 * t - 256)
                        hi = min(2048, 128 * t + 384)
                        wait(e, "mx", b_mx + t + 1)
                        e.activation(probs[:, lo:hi], pA[:, lo:hi], AF.Exp,
                                     bias=stats[:, t, 6:7])
                        inc(e.activation(stats[:, t, 5:6], stats[:, t, 6:7], AF.Exp), "ex", 1)
                        for i in range(nk):
                            wait(e, "tr", b_tr + n_tr + 1)
                            inc(e.activation(atT(t, i), pT[:, n_tr % 2, 0:128], AF.Copy), "at", 1)
                            n_tr += 1
                    if t >= 1:
                        postact(t - 1)

    def emit_ffn0(blk, hooks=None):
        take_snap()
        b_sm, b_mx, b_fg = snap["sm"], snap["mx"], snap["fg"]
        b_tt, b_tp, b_te = snap["tt"], snap["tp"], snap["te"]
        b_oo, b_pl, b_s2f, b_vr2, b_rr = snap["oo"], snap["pl"], snap["s2p"], snap["vr2"], snap["rr"]
        b_ze = snap["ze"]

        base_ps = lambda t: pB[:, 0:1024] if t % 2 == 0 else pA[:, 1024:2048]
        sm_ap = lambda t: pA[:, (t // 8) * 512 + (t % 8) * 16:(t // 8) * 512 + (t % 8) * 16 + 16]
        t64T_ap = lambda t: w2[0:64, 0, 1024 + t * 128: 1152 + t * 128]
        t64_ap = lambda t: w2[:, 0, 3072 + t * 64: 3136 + t * 64]

        @blk.tensor
        def _(e):
            barrier(e, skip=("ev",))
            wait_mark(e, "bw", ["wl"])
            wait_mark(e, "bb", ["wl"])
            wait_mark(e, "wlb", ["wl"])
            b_evp = snap["ev"] - NT
            for t in range(NT):
                wait(e, "ev", b_evp + NT - 16 + t + 1)
                ins = None
                for dc in range(DC):
                    ins = e.matmul(sm_ap(t), xbf[:, dc, t * 128:(t + 1) * 128],
                                   wsm[:, dc, :], start=(dc == 0), stop=False)
                ins = e.matmul(sm_ap(t), ones_row[:, :], rb_row[:, :], start=False, stop=True)
                inc(ins, "sm", 1)
            for t in range(NT):
                wait(e, "tt", b_tt + t + 1)
                if t >= 2:
                    wait(e, "te", b_te + t - 1)
                inc(e.transpose(pT[0:64, t % 2, 0:128], t64_ap(t), ident[:, :]), "tp", 1)
            for t in range(NT):
                if t >= 2:
                    wait(e, "ze", b_ze + t - 1)
                bp = base_ps(t)
                for og in range(2):
                    for dc in range(DC):
                        e.matmul(bp[:, og * 512:(og + 1) * 512],
                                 xbf[:, dc, t * 128:(t + 1) * 128],
                                 w2[:, 1, dc * 1024 + og * 512: dc * 1024 + (og + 1) * 512],
                                 start=(dc == 0), stop=False)
                    e.matmul(bp[:, og * 512:(og + 1) * 512], ones_row[:, :],
                             brow[:, og * 512:(og + 1) * 512], start=False, stop=False)
                wait(e, "te", b_te + t + 1)
                ins = None
                for og in range(2):
                    ins = e.matmul(bp[:, og * 512:(og + 1) * 512], t64T_ap(t),
                                   wlb[:, og * 512:(og + 1) * 512], start=False, stop=True)
                inc(ins, "oo", 1)
            set_mark("l0f_mm")

        @blk.vector
        def _(e):
            e = wrapv(e)
            barrier(e)
            gall_s = lambda t: u8[:, 1024:2048].bitcast(F32)[:, t * 16:t * 16 + 8]
            g2f_s = lambda t: u8[:, 1024:2048].bitcast(F32)[:, t * 16 + 8:t * 16 + 16]
            gbf_s = lambda t: u8[:, 2048 + 8 * t:2048 + 8 * t + 8]
            for t in range(NT):
                wait(e, "sm", b_sm + (8 if t < 8 else 16))
                inc(e.reduce_max(stats[:, t, 6:7], sm_ap(t)[:, 0:8], axis=AX.X, negate=True), "mx", 1)
            for t in range(NT):
                wait(e, "fg", b_fg + t + 1)
                e.reduce_sum(stats[:, t, 0:1], gall_s(t), axis=AX.X)
            for t in range(NT):
                e.reciprocal(stats[:, t, 7:8], stats[:, t, 0:1])
            for t in range(NT):
                e.tensor_scalar_mul(gbf_s(t), gall_s(t), stats[:, t, 7:8])
            for t in range(NT):
                e.tensor_scalar_mul(g2f_s(t), gbf_s(t), 2.0)
            for t in range(NT):
                ins = None
                mid = sm_ap(t)[:, 8:16]
                for ex in range(8):
                    ins = e.tensor_single_scalar(t64_ap(t)[:, ex * 8:(ex + 1) * 8], mid,
                                                 g2f_s(t)[:, ex:ex + 1], ALU.mult)
                inc(ins, "tt", 1)

            def postln(j):
                bp = vbuf[:, j, :]
                wait(e, "ze", b_ze + j + 1)
                wait(e, "s2p", b_s2f + j + 1)
                e.tensor_scalar_mul(stats[:, j, 2:3], stats[:, j, 0:1], INV_D)
                e.tensor_mul(stats[:, j, 6:7], stats[:, j, 2:3], stats[:, j, 2:3])
                inc(e.tensor_scalar(stats[:, j, 3:4], stats[:, j, 1:2], INV_D,
                                    stats[:, j, 6:7], ALU.mult, ALU.subtract), "vr2", 1)
                wait(e, "rr", b_rr + j + 1)
                e.tensor_mul(stats[:, j, 5:6], stats[:, j, 2:3], stats[:, j, 4:5])
                e.tensor_add(macc[:, j:j + 1], macc[:, j:j + 1], stats[:, j, 5:6])
                e.scalar_tensor_tensor(s1x[:, j:j + 1], stats[:, j, 5:6], float(D),
                                       s1x[:, j:j + 1], ALU.mult, ALU.add)
                inc(e.scalar_tensor_tensor(xs[:, j, :], bp, stats[:, j, 4:5],
                                           xs[:, j, :], ALU.mult, ALU.add), "pl", 1)
                if hooks is not None:
                    hooks["dve"](e, j)

            for j in range(NT):
                postln(j)

        @blk.scalar
        def _(e):
            gall_s = lambda t: u8[:, 1024:2048].bitcast(F32)[:, t * 16:t * 16 + 8]
            for t in range(NT):
                wait(e, "mx", b_mx + t + 1)
                inc(e.activation(gall_s(t), sm_ap(t)[:, 0:8], AF.Exp, bias=stats[:, t, 6:7]), "fg", 1)
            for t in range(NT):
                wait(e, "tp", b_tp + t + 1)
                inc(e.activation(t64T_ap(t), pT[0:64, t % 2, 0:128], AF.Copy), "te", 1)

            def postact(j):
                wait(e, "oo", b_oo + j + 1)
                inc(e.activation(vbuf[:, j, :], base_ps(j), AF.Identity,
                                 accum_out=stats[:, j, 0:1]), "ze", 1)
                inc(e.activation(xbt[:, 0, :], vbuf[:, j, :], AF.Square,
                                 accum_out=stats[:, j, 1:2]), "s2p", 1)
                wait(e, "vr2", b_vr2 + j + 1)
                e.activation(stats[:, j, 6:7], stats[:, j, 3:4], AF.Ln, bias=cst[:, 0:1])
                inc(e.activation(stats[:, j, 4:5], stats[:, j, 6:7], AF.Exp, scale=-0.5), "rr", 1)
                if hooks is not None:
                    hooks["act"](e, j)

            for j in range(NT):
                postact(j)

        @blk.gpsimd
        def _(e):
            wait_mark(e, "l0f_mm", ["oo"])
            inc(e.dma_start(w2[:, 0, :].rearrange("p (dc o) -> p dc o", dc=8), dr["l1wk"].rearrange("(dc p) o -> p dc o", p=128)), "wl", 16)
            set_mark("wk1")
            inc(e.dma_start(w2[:, 1, :].rearrange("p (dc o) -> p dc o", dc=8), dr["l1wv"].rearrange("(dc p) o -> p dc o", p=128)), "wl", 16)
            set_mark("wv1")

    def emit_ffn1_prep(blk):
        take_snap()
        b_cn, b_cb, b_tp, b_k1 = snap["cn"], snap["cb"], snap["tp"], snap["k1"]

        @blk.scalar
        def _(e):
            wait(e, "ld", 32)
            e.activation(et_bf[:, :], tmp_r[:, :], AF.Exp)
            e.activation(xbt[0:16, 0, 0:128], cent[:, :], AF.Square, accum_out=csc[:, 0:1])
            e.activation(csc[:, 1:2], csc[:, 0:1], AF.Ln, bias=cst[0:16, 1:2])
            inc(e.activation(csc[:, 0:1], csc[:, 1:2], AF.Exp, scale=-0.5), "cn", 1)

        @blk.vector
        def _(e):
            e = wrapv(e)
            wait(e, "cn", b_cn + 1)
            inc(e.tensor_scalar_mul(cn_bf, cent[:, :], csc[:, 0:1]), "cb", 1)

        @blk.tensor
        def _(e):
            wait(e, "cb", b_cb + 1)
            inc(e.transpose(pT[:, 0, 0:16], cn_bf, ident[0:16, 0:16]), "tp", 1)
            inc(e.matmul(pB[:, 0:1], ones_row[:, :], et_bf[:, :], start=True, stop=True), "k1", 1)

        @blk.vector
        def _(e):
            e = wrapv(e)
            wait(e, "tp", b_tp + 1)
            e.tensor_copy(cnT[:, :], pT[:, 0, 0:16])
            wait(e, "k1", b_k1 + 1)
            inc(e.tensor_copy(etb[:, :], pB[:, 0:1]), "pp", 1)

    def emit_ffn1(blk):
        take_snap()
        b_zg, b_zev = snap["zg"], snap["zev"]
        b_sm, b_rz, b_v3, b_fg, b_wd = snap["sm"], snap["rz"], snap["v3"], snap["fg"], snap["wd"]
        b_hg, b_ge, b_og, b_zz, b_ew = snap["hg"], snap["ge"], snap["og"], snap["zz"], snap["ew"]
        b_s2, b_vr2, b_rr = snap["s2p"], snap["vr2"], snap["rr"]

        b_evp = snap["ev"] - NT

        @blk.tensor
        def _(e):
            wait_mark(e, "wrw", ["wl"])
            wait(e, "pp", snap["pp"])
            ins = None
            for kg in range(4):
                wait(e, "ev", b_evp + 4 * (kg + 1))
                for dc in range(DC):
                    ins = e.matmul(pA[:, kg * 512:(kg + 1) * 512],
                                   w2[:, 1, dc * 128:(dc + 1) * 128],
                                   xbf[:, dc, kg * 512:(kg + 1) * 512],
                                   start=(dc == 0), stop=(dc == DC - 1))
            inc(ins, "zg", 1)

        @blk.scalar
        def _(e):
            wait(e, "zg", b_zg + 1)
            e.activation(z_bf, pA[:, :], AF.Copy)
            inc(e.activation(z2_bf, pA[:, :], AF.Square), "zev", 1)

        @blk.tensor
        def _(e):
            wait(e, "zev", b_zev + 1)
            for t in range(NT):
                po = t * 32
                e.matmul(pB[:, po:po + 1], z2_bf[:, t * 128:(t + 1) * 128], onescol[:, :],
                         start=True, stop=True)
                inc(e.matmul(pB[:, po + 1:po + 17], z_bf[:, t * 128:(t + 1) * 128], cnT[:, :],
                             start=True, stop=True), "sm", 1)

        @blk.scalar
        def _(e):
            wait(e, "sm", b_sm + NT)
            e.activation(rstdA.rearrange("p (t c) -> p t c", c=1),
                         pB[:, 0:512].rearrange("p (t c) -> p t c", c=32)[:, :, 0:1],
                         AF.Ln, bias=cst[:, 1:2])
            inc(e.activation(rstd16, rstdA, AF.Exp, scale=-0.5), "rz", 1)

        @blk.vector
        def _(e):
            e = wrapv(e)
            wait(e, "rz", b_rz + 1)
            e.tensor_scalar(rscl16, rstd16, etb[:, :], ISQ128, ALU.mult, ALU.mult)
            for t in range(NT):
                wait(e, "sm", b_sm + t + 1)
                inc(e.tensor_scalar_mul(lsc_all[:, t * 16:(t + 1) * 16],
                                        pB[:, t * 32 + 1:t * 32 + 17],
                                        rscl16[:, t:t + 1]), "v3", 1)
            wait(e, "fg", b_fg + 1)
            for t in range(NT):
                e.reduce_sum(rsum16[:, t:t + 1], s16all[:, t * 16:(t + 1) * 16], axis=AX.X)
            e.reciprocal(rsum16[:, :], rsum16[:, :])
            for t in range(NT):
                e.reduce_max(m1_16[:, t:t + 1], s16all[:, t * 16:(t + 1) * 16], axis=AX.X)
            for t in range(NT):
                e.tensor_single_scalar(exm[:, t * 16:(t + 1) * 16],
                                       s16all[:, t * 16:(t + 1) * 16],
                                       m1_16[:, t:t + 1], ALU.is_lt)
            for t in range(NT):
                e.tensor_mul(exm[:, t * 16:(t + 1) * 16], exm[:, t * 16:(t + 1) * 16],
                             s16all[:, t * 16:(t + 1) * 16])
            for t in range(NT):
                e.reduce_max(m1_16[:, t:t + 1], exm[:, t * 16:(t + 1) * 16], axis=AX.X)
            for t in range(NT):
                e.tensor_single_scalar(exm[:, t * 16:(t + 1) * 16],
                                       s16all[:, t * 16:(t + 1) * 16],
                                       m1_16[:, t:t + 1], ALU.is_ge)
            for t in range(NT):
                inc(e.scalar_tensor_tensor(w_all[:, t, :], exm[:, t * 16:(t + 1) * 16],
                                           rsum16[:, t:t + 1],
                                           s16all[:, t * 16:(t + 1) * 16],
                                           ALU.mult, ALU.mult), "wd", 1)

        @blk.gpsimd
        def _(e):
            wait_mark(e, "l1att", ["oo"])
            inc(e.dma_start(wout[:, :, :], dr["l1wo"][0].rearrange("(hc p) o -> p hc o", p=128)), "wl", 16)
            set_mark("wo0")
            for n in range(1, 16):
                if n == 2:
                    wait(e, "sm", b_sm + NT)
                elif n >= 3:
                    wait(e, "hg", b_hg + (n - 2) * 16)
                if n >= 2:
                    inc(e.dma_start(we_dst(n),
                                    dr["l1we"][n].rearrange("(dc p) h -> p dc h", p=128)), "wl", 16)
                set_mark(f"we{n}x")
                wait(e, "ew", b_ew + n)
                inc(e.dma_start(wout[:, :, :],
                                dr["l1wo"][n].rearrange("(hc p) o -> p hc o", p=128)), "wl", 16)
                set_mark(f"wld{n}")

        @blk.tensor
        def _(e):
            wait(e, "zev", b_zev + 1)
            for n in range(16):
                if n == 0:
                    wait_mark(e, "we0", ["wl"])
                elif n == 1:
                    wait_mark(e, "we1", ["wl"])
                else:
                    wait_mark(e, f"we{n}x", ["wl"])
                for hc in range(4):
                    for g in range(4):
                        idx = n * 16 + hc * 4 + g
                        if n == 0 and hc == 0:
                            wait(e, "ev", b_evp + 4 * (g + 1))
                        if idx >= 4:
                            wait(e, "ge", b_ge + idx - 3)
                        ins = None
                        for dc in range(DC):
                            ins = e.matmul(pA[:, g * 512:(g + 1) * 512],
                                           we_ap(n, dc * 512 + hc * 128, dc * 512 + (hc + 1) * 128),
                                           xbf[:, dc, g * 512:(g + 1) * 512],
                                           start=(dc == 0), stop=(dc == DC - 1))
                        inc(ins, "hg", 1)
                wait_mark(e, "wo0" if n == 0 else f"wld{n}", ["wl"])
                if n == 0:
                    wait(e, "wd", b_wd + NT)
                for t in range(NT):
                    idx = n * 16 + t
                    if idx >= 2:
                        wait(e, "zz", b_zz + idx - 1)
                    po = (t % 2) * 512
                    ins = None
                    for hc in range(4):
                        if t == 0:
                            wait(e, "ge", b_ge + n * 16 + hc * 4 + 4)
                        ins = e.matmul(pB[:, po:po + 64],
                                       w2[:, 1, hc * 2048 + t * 128: hc * 2048 + (t + 1) * 128],
                                       wout[:, hc, :],
                                       start=(hc == 0), stop=(hc == 3))
                    inc(ins, "og", 1)
                inc(e.nop(), "ew", 1)

        @blk.scalar
        def _(e):
            for n in range(16):
                for hc in range(4):
                    for g in range(4):
                        idx = n * 16 + hc * 4 + g
                        if idx == 4:
                            wait(e, "v3", b_v3 + NT)
                            inc(e.activation(s16all, lsc_all, AF.Exp), "fg", 1)
                        wait(e, "hg", b_hg + idx + 1)
                        inc(e.activation(w2[:, 1, hc * 2048 + g * 512: hc * 2048 + (g + 1) * 512],
                                         pA[:, g * 512:(g + 1) * 512], AF.Gelu), "ge", 1)

        @blk.vector
        def _(e):
            e = wrapv(e)
            for n in range(16):
                for t in range(NT):
                    idx = n * 16 + t
                    wait(e, "og", b_og + idx + 1)
                    po = (t % 2) * 512
                    inc(e.tensor_scalar_mul(z_slot(t, n * 64, (n + 1) * 64),
                                            pB[:, po:po + 64], w_all[:, t, n:n + 1]), "zz", 1)

        b_zea, b_vr = snap["ze"], snap["vr"]

        def z_lnexp(e, k):
            wait(e, "vr2", b_vr2 + k + 1)
            e.activation(stats[:, k, 6:7], stats[:, k, 3:4], AF.Ln, bias=cst[:, 0:1])
            inc(e.activation(stats[:, k, 4:5], stats[:, k, 6:7], AF.Exp, scale=-0.5), "rr", 1)

        def z_apply(e, k):
            wait(e, "vr", b_vr + k + 1)
            inc(e.activation(vbuf[:, k % 4, :], z_slot(k, 0, 1024), AF.Identity,
                             bias=stats[:, k, 5:6], scale=stats[:, k, 4:5]), "ze", 1)

        @blk.scalar
        def _(e):
            for t in range(NT):
                wait(e, "zz", b_zz + 15 * NT + t + 1)
                inc(e.activation(xbt[:, 0, :], z_slot(t, 0, 1024), AF.Square,
                                 accum_out=stats[:, t, 1:2]), "s2p", 1)
                if t >= 1:
                    z_lnexp(e, t - 1)
                if t >= 2:
                    z_apply(e, t - 2)
            z_lnexp(e, NT - 1)
            z_apply(e, NT - 2)
            z_apply(e, NT - 1)

        @blk.vector
        def _(e):
            e = wrapv(e)

            def z_mr(e, k):
                wait(e, "rr", b_rr + k + 1)
                e.tensor_mul(stats[:, k, 5:6], stats[:, k, 2:3], stats[:, k, 4:5])

            def z_bias(e, k):
                inc(e.tensor_scalar(stats[:, k, 5:6], stats[:, k, 5:6], macc[:, k:k + 1],
                                    -1.0, ALU.add, ALU.mult), "vr", 1)

            def z_add(e, k):
                wait(e, "ze", b_zea + k + 1)
                inc(e.tensor_tensor(xs[:, k, :], xs[:, k, :], vbuf[:, k % 4, :],
                                    ALU.add), "pl", 1)

            for t in range(NT):
                wait(e, "zz", b_zz + 15 * NT + t + 1)
                e.reduce_sum(stats[:, t, 0:1], z_slot(t, 0, 1024), axis=AX.X)
                if t >= 1:
                    z_mr(e, t - 1)
                e.tensor_scalar_mul(stats[:, t, 2:3], stats[:, t, 0:1], INV_D)
                if t >= 1:
                    z_bias(e, t - 1)
                e.tensor_mul(stats[:, t, 6:7], stats[:, t, 2:3], stats[:, t, 2:3])
                if t >= 2:
                    z_add(e, t - 2)
                wait(e, "s2p", b_s2 + t + 1)
                inc(e.tensor_scalar(stats[:, t, 3:4], stats[:, t, 1:2], INV_D,
                                    stats[:, t, 6:7], ALU.mult, ALU.subtract), "vr2", 1)
            z_mr(e, NT - 1)
            z_add(e, NT - 2)
            z_bias(e, NT - 1)
            z_add(e, NT - 1)

    hooks_next = None
    with nc.Block() as blk:
        emit_setup(blk)
        for part in parts:
            if part == 'pre':
                emit_preln(blk)
            elif part == 'prekv':
                emit_preln(blk)
                emit_proj_kvq(blk, 0)
            elif part == 0:
                emit_preln(blk, prev=None)
                emit_proj_kvq(blk, 0)
                hk = make_preln_hooks_v2() if 1 in parts else None
                emit_attn_core(blk, 0, first_attn=True, hooks=hk)
                hooks_next = hk
            elif part == 1:
                emit_ffn1_prep(blk)
                emit_preln(blk, prev="l0a", hooks=hooks_next)
                hk = make_preln_hooks_v2() if 2 in parts else None
                emit_ffn0(blk, hooks=hk)
                hooks_next = hk
            elif part == 2:
                emit_preln(blk, prev="ffn0", hooks=hooks_next)
                emit_build_mask(blk)
                emit_proj_kvq(blk, 1)
                hk = make_preln_hooks() if 3 in parts else None
                emit_attn_core(blk, 1, first_attn=False, hooks=hk)
                hooks_next = hk
            elif part == 3:
                emit_preln(blk, prev="l1a", hooks=hooks_next)
                emit_ffn1(blk)

        take_snap()

        tap_map = {"xbf": xbf, "kbuf": kbuf, "vbuf": vbuf, "u8": u8, "stats": stats,
                   "w2": w2, "m640": m640, "xbt": xbt}

        @blk.sync
        def _(e):
            p_plz = snap["pl"] - NT
            o_dst = out.rearrange("(t p) d -> p t d", p=128)
            for t in range(NT):
                wait(e, "pl", p_plz + t + 1)
                inc(e.dma_start(o_dst[:, t, :], xs[:, t, :]), "st", 16)
            for tname in taps:
                tsrc = tap_map[tname]
                tdr = nc.dram_tensor("tap_" + tname, list(tsrc.shape), tsrc.dtype,
                                     kind="ExternalOutput")
                inc(e.dma_start(tdr.ap(), tsrc.ap() if hasattr(tsrc, "ap") else tsrc), "st", 16)
            e.wait_ge(sem("st"), cnt["st"])

    es.close()
    return nc



def _prep_weights(kw):
    asc = np.ascontiguousarray
    f = lambda a: np.asarray(a, np.float32)
    d = {}
    d["l0wq"] = asc(f(kw["L0_attn_Wq"]).T)
    d["l0wk"] = asc(f(kw["L0_attn_Wk"]).T)
    d["l0wv"] = asc(f(kw["L0_attn_Wv"]).T)
    d["l0bq"] = asc(f(kw["L0_attn_bq"]).reshape(8, 128).T)
    d["l0bk"] = asc(f(kw["L0_attn_bk"]).reshape(8, 128).T)
    d["l0bv"] = f(kw["L0_attn_bv"]).reshape(1, 1024)
    d["l0bw"] = asc(f(kw["L0_ffn_base_w"]).T)
    d["l0bb"] = f(kw["L0_ffn_base_b"]).reshape(1, 1024)
    d["l0sm"] = asc(np.concatenate([f(kw["L0_ffn_router_w"]).T, f(kw["L0_ffn_lora_A"])], axis=1))
    d["l0rb"] = np.concatenate([f(kw["L0_ffn_router_b"]).reshape(1, 8),
                            np.zeros((1, 8), np.float32)], axis=1)
    d["l0lb"] = asc(f(kw["L0_ffn_lora_B"]).reshape(64, 1024))
    d["l1wq"] = asc(f(kw["L1_attn_WQ"]))
    d["l1wk"] = asc(f(kw["L1_attn_WK"]))
    d["l1wv"] = asc(f(kw["L1_attn_WV"]))
    d["l1rw"] = asc(f(kw["L1_ffn_w_router"]).T)
    d["l1ce"] = asc(f(kw["L1_ffn_head_centroids"]))
    d["l1tm"] = f(kw["L1_ffn_temperature"]).reshape(1, 1)
    d["l1we"] = asc(f(kw["L1_ffn_input_experts"]))
    d["l1wo"] = asc(f(kw["L1_ffn_output_experts"]))
    return d


def kernel(**inputs):
    x = np.asarray(inputs["L0_attn_x"], np.float32)
    w = _prep_weights(inputs)
    nc = build_program(ALL_PARTS)
    in_maps = []
    for cidx in range(NCORES):
        m = dict(w)
        m["x_in"] = np.ascontiguousarray(x[cidx])
        in_maps.append(m)
    res = run_bass_kernel_spmd(nc, in_maps, core_ids=list(range(NCORES)))
    return np.stack([r["out"] for r in res.results], axis=0)


# version 37
# speedup vs baseline: 1.0042x; 1.0042x over previous
import math
from collections import defaultdict
from contextlib import ExitStack

import numpy as np
import concourse.bass as bass
import concourse.mybir as mybir
from concourse.bass_utils import run_bass_kernel_spmd

F32 = mybir.dt.float32
BF16 = mybir.dt.bfloat16
AF = mybir.ActivationFunctionType
ALU = mybir.AluOpType
AX = mybir.AxisListType

NCORES = 8
T = 2048
NT = 16
D = 1024
DC = 8
EPS = 1e-5
INV_D = 1.0 / D
ISQ128 = 1.0 / math.sqrt(128.0)

ALL_PARTS = (0, 1, 2, 3)


_V_OPS = {
    "tensor_copy", "tensor_tensor", "tensor_add", "tensor_sub", "tensor_mul",
    "tensor_max", "tensor_scalar", "tensor_single_scalar", "tensor_scalar_add",
    "tensor_scalar_sub", "tensor_scalar_mul", "tensor_scalar_max", "tensor_scalar_min",
    "tensor_relu", "tensor_reduce", "reduce_max", "reduce_sum", "scalar_tensor_tensor",
    "tensor_tensor_scan", "memset", "memzero", "reciprocal", "select", "transpose",
    "copy_predicated",
}


_DT_SIZE = {mybir.dt.float32: 4, mybir.dt.bfloat16: 2, mybir.dt.float16: 2,
            mybir.dt.int32: 4, mybir.dt.int8: 1}


def _ap_rng(a):
    try:
        name = a.tensor.name
        sz = _DT_SIZE.get(a.dtype, 4)
        span = 0
        for step, count in list(a.ap)[1:]:
            span += int(step) * (int(count) - 1)
        return (name, int(a.offset) * sz, (int(a.offset) + span + 1) * sz)
    except Exception:
        try:
            return (a.tensor.name, 0, 1 << 60)
        except Exception:
            return None


def _rngs_overlap(r1, r2):
    for a in r1:
        for b in r2:
            if a and b and a[0] == b[0] and a[1] < b[2] and b[1] < a[2]:
                return True
    return False


class _VSync:

    def __init__(self, e, sem, cnt, apcls, state):
        self._e = e
        self._sem = sem
        self._cnt = cnt
        self._ap = apcls
        self._st = state

    def note_sync(self, ins, semh, val):
        if self._st.get("last") is ins:
            self._st["sync"] = (semh, val)

    def __getattr__(self, name):
        fn = getattr(self._e, name)
        if name not in _V_OPS:
            return fn

        def wrapped(*args, **kw):
            aps = [a for a in args if isinstance(a, self._ap)]
            outs = [_ap_rng(aps[0])] if aps else []
            ins_n = [_ap_rng(a) for a in aps[1:]]
            if "accum_out" in kw and isinstance(kw["accum_out"], self._ap):
                outs.append(_ap_rng(kw["accum_out"]))
            st = self._st
            if _rngs_overlap(ins_n, st["w"]) or _rngs_overlap(outs, st["r"] + st["w"]):
                if st.get("last") is not None:
                    if st.get("sync") is None:
                        st["last"].then_inc(self._sem, 1)
                        self._cnt[0] += 1
                        st["sync"] = (self._sem, self._cnt[0])
                    self._e.wait_ge(*st["sync"])
            r = fn(*args, **kw)
            st["r"], st["w"] = ins_n, outs
            st["last"], st["sync"] = r, None
            return r

        return wrapped


def build_program(parts=ALL_PARTS, taps=()):
    nc = bass.Bass("TRN2", target_bir_lowering=False, detect_race_conditions=False)
    es = ExitStack()
    S = {}
    cnt = defaultdict(int)
    mark = {}
    snap = defaultdict(int)

    vv_cnt = [0]
    vv_state = {"r": [], "w": [], "last": None, "sync": None}

    def wrapv(e):
        return _VSync(e, sem("vv"), vv_cnt, bass.AP, vv_state)

    def sem(name):
        if name not in S:
            S[name] = es.enter_context(nc.semaphore(name))
        return S[name]

    def inc(ins, name, k=1):
        ins.then_inc(sem(name), k)
        cnt[name] += k
        if vv_state.get("last") is ins:
            vv_state["sync"] = (sem(name), cnt[name])
        return ins

    def wait(e, name, v):
        if v > 0:
            e.wait_ge(sem(name), v)

    def set_mark(name):
        mark[name] = dict(cnt)

    def wait_mark(e, name, sems):
        if name in mark:
            for s in sems:
                wait(e, s, mark[name].get(s, 0))

    def take_snap():
        snap.clear()
        snap.update(cnt)

    _BARRIER_SKIP = {"wl", "st", "lx"}

    def barrier(e, skip=()):
        for name, v in list(snap.items()):
            if name not in _BARRIER_SKIP and name not in skip:
                wait(e, name, v)

    dr = {}
    for name, shape in [
        ("x_in", (T, D)),
        ("l0wq", (D, D)), ("l0wk", (D, D)), ("l0wv", (D, D)),
        ("l0bq", (128, 8)), ("l0bk", (128, 8)), ("l0bv", (1, D)),
        ("l0bw", (D, D)), ("l0bb", (1, D)),
        ("l0sm", (D, 16)), ("l0rb", (1, 16)), ("l0lb", (64, D)),
        ("l1wq", (D, D)), ("l1wk", (D, D)), ("l1wv", (D, D)),
        ("l1rw", (D, 128)), ("l1ce", (16, 128)), ("l1tm", (1, 1)),
        ("l1we", (16, D, 512)), ("l1wo", (16, 512, 64)),
    ]:
        dr[name] = nc.dram_tensor(name, list(shape), F32, kind="ExternalInput")
    out = nc.dram_tensor("out", [T, D], F32, kind="ExternalOutput")

    sb = lambda n, sh, dt: es.enter_context(nc.sbuf_tensor(n, list(sh), dt))
    psb = lambda n, sh, dt: es.enter_context(nc.psum_tensor(n, list(sh), dt))

    xs = sb("xs", (128, NT, D), F32)
    xbf = sb("xbf", (128, DC, T), BF16)
    kbuf = sb("kbuf", (128, DC, T), BF16)
    vbuf = sb("vbuf", (128, NT, D), BF16)
    w2 = sb("w2", (128, 2, 8192), BF16)
    u8 = sb("u8", (128, 4096), BF16)
    m640 = sb("m640", (128, 1664), BF16)
    xbt = sb("xbt", (128, 1, D), BF16)
    stats = sb("stats", (128, NT, 8), F32)
    cst = sb("cst", (128, 2), F32)
    macc = sb("macc", (128, NT), F32)
    s1x = sb("s1x", (128, NT), F32)
    ident = sb("ident", (128, 128), BF16)
    ones_row = sb("ones_row", (1, 128), BF16)
    onescol = sb("onescol", (128, 1), BF16)
    bqc = sb("bqc", (128, 8), BF16)
    bkc = sb("bkc", (128, 8), BF16)
    wsm = sb("wsm", (128, DC, 16), BF16)
    rb_row = sb("rb_row", (1, 16), BF16)
    gsc = sb("gsc", (128, 8), F32)
    g_bf = sb("g_bf", (128, 8), BF16)
    g2f = sb("g2f", (128, 8), F32)
    cent = sb("cent", (16, 128), F32)
    csc = sb("csc", (16, 2), F32)
    cnT = sb("cnT", (128, 16), BF16)
    tmp_r = sb("tmp_r", (1, 1), F32)
    et_bf = sb("et_bf", (1, 1), BF16)
    etb = sb("etb", (128, 1), F32)
    lsc = sb("lsc", (128, 16), F32)
    s16a = sb("s16a", (128, 16), F32)
    s16b = sb("s16b", (128, 16), F32)

    wlb = w2[0:64, 0, 0:1024]
    t64T = w2[0:64, 0, 1024:1152]
    t64 = w2[:, 0, 1152:1216]

    pA = psb("pA", (128, 2048), F32)
    pB = psb("pB", (128, 1024), F32)
    pT = psb("pT", (128, 2, 1024), BF16)

    probs = u8[:, 0:2048]
    attnT = lambda i: u8[:, 2048 + i * 128: 2048 + (i + 1) * 128]
    z_bf = u8[:, 0:2048]
    z2_bf = u8[:, 2048:4096]
    cn_bf = m640[0:16, 256:384]

    def we_ap(n, lo, hi):
        if n % 3 < 2:
            return w2[:, 0, (n % 3) * 4096 + lo:(n % 3) * 4096 + hi]
        return u8[:, lo:hi]

    def we_dst(n):
        if n % 3 < 2:
            return w2[:, 0, (n % 3) * 4096:(n % 3) * 4096 + 4096].rearrange(
                "p (dc h) -> p dc h", dc=8)
        return u8[:, 0:4096].rearrange("p (dc h) -> p dc h", dc=8)
    brow = m640[0:1, 640:1664]
    mask_rel = m640[:, 0:640]
    wout = m640[:, 0:256].rearrange("p (hc o) -> p hc o", hc=4)
    w_all = m640[:, 256:768].bitcast(F32).rearrange("p (t n) -> p t n", t=NT)
    rv = vbuf[:, 14, :].bitcast(F32)
    rv2 = vbuf[:, 15, :].bitcast(F32)
    lsc_all = rv[:, 0:256]
    s16all = rv[:, 256:512]
    exm = rv2[:, 0:256]
    rstdA = rv2[:, 256:272]
    rstd16 = rv2[:, 272:288]
    rscl16 = rv2[:, 288:304]
    rsum16 = rv2[:, 304:320]
    m1_16 = rv2[:, 320:336]

    def z_slot(t, lo, hi):
        off = (t % 2) * 1024
        return kbuf[:, t // 2, off + lo: off + hi]

    def wslot(i):
        return w2[:, i, :]


    def emit_setup(blk):
        take_snap()

        @blk.sync
        def _(e):
            inc(e.dma_start(cent[:, :], dr["l1ce"][:, :]), "ld", 16)
            inc(e.dma_start(tmp_r[:, :], dr["l1tm"][:, :]), "ld", 16)
            x_src = dr["x_in"].rearrange("(t p) d -> p t d", p=128)
            for c in range(8):
                inc(e.dma_start(xs[:, c * 2:(c + 1) * 2, :], x_src[:, c * 2:(c + 1) * 2, :]),
                    "lx", 16)

        @blk.gpsimd
        def _(e):
            inc(e.dma_start(bqc[:, :], dr["l0bq"][:, :]), "wl", 16)
            inc(e.dma_start(bkc[:, :], dr["l0bk"][:, :]), "wl", 16)
            inc(e.dma_start(w2[:, 0, :].rearrange("p (dc o) -> p dc o", dc=8), dr["l0wk"].rearrange("(dc p) o -> p dc o", p=128)), "wl", 16)
            set_mark("wk0")
            inc(e.dma_start(w2[:, 1, :].rearrange("p (dc o) -> p dc o", dc=8), dr["l0wv"].rearrange("(dc p) o -> p dc o", p=128)), "wl", 16)
            set_mark("wv0")
            inc(e.dma_start(brow, dr["l0bv"][:, :]), "wl", 16)
            set_mark("bv")
            inc(e.dma_start(wsm[:, :, :], dr["l0sm"].rearrange("(dc p) o -> p dc o", p=128)), "wl", 16)
            inc(e.dma_start(rb_row[:, :], dr["l0rb"][:, :]), "wl", 16)
            set_mark("smalls")
            e.wait_ge(sem("dv"), 1)
            inc(e.affine_select(ident[:, :], ident[:, :], [[1, 128]], ALU.is_equal, 0.0,
                                base=0, channel_multiplier=-1), "gp", 1)

        @blk.vector
        def _(e):
            e = wrapv(e)
            e.memset(ones_row[:, :], 1.0)
            e.memset(onescol[:, :], 1.0)
            e.memset(cst[:, 0:1], EPS)
            e.memset(cst[:, 1:2], 1e-24)
            inc(e.memset(ident[:, :], 1.0), "dv", 1)

    def make_preln_hooks():
        base = {"hvr": cnt["hvr"], "hrr": cnt["hrr"]}

        def dve_stats(e, j):
            e.scalar_tensor_tensor(xbt[:, 0, :], xs[:, j, :], 1.0, xs[:, j, :],
                                   ALU.mult, ALU.mult, accum_out=stats[:, j, 1:2])
            e.tensor_scalar_mul(stats[:, j, 2:3], s1x[:, j:j + 1], INV_D)
            e.tensor_mul(stats[:, j, 7:8], stats[:, j, 2:3], stats[:, j, 2:3])
            inc(e.tensor_scalar(stats[:, j, 3:4], stats[:, j, 1:2], INV_D,
                                stats[:, j, 7:8], ALU.mult, ALU.subtract), "hvr", 1)

        def act_stats(e, j):
            wait(e, "hvr", base["hvr"] + j + 1)
            e.activation(stats[:, j, 6:7], stats[:, j, 3:4], AF.Ln, bias=cst[:, 0:1])
            inc(e.activation(stats[:, j, 4:5], stats[:, j, 6:7], AF.Exp, scale=-0.5), "hrr", 1)

        return {"dve": dve_stats, "act": act_stats, "base": base}

    def make_preln_hooks_v2(do_apply=True):
        base = {"hsq": cnt["hsq"], "hvr": cnt["hvr"], "hrr": cnt["hrr"],
                "hap": cnt["hap"], "pl": cnt["pl"]}
        p_pl = base["pl"]

        def sq(e, k):
            wait(e, "pl", p_pl + k + 1)
            inc(e.activation(xbt[:, 0, :], xs[:, k, :], AF.Square,
                             accum_out=stats[:, k, 1:2]), "hsq", 1)

        def lnexp(e, k):
            wait(e, "hvr", base["hvr"] + k + 1)
            e.activation(stats[:, k, 6:7], stats[:, k, 3:4], AF.Ln, bias=cst[:, 0:1])
            inc(e.activation(stats[:, k, 4:5], stats[:, k, 6:7], AF.Exp, scale=-0.5), "hrr", 1)

        def var(e, k):
            wait(e, "hsq", base["hsq"] + k + 1)
            inc(e.tensor_scalar(stats[:, k, 3:4], stats[:, k, 1:2], INV_D,
                                stats[:, k, 7:8], ALU.mult, ALU.subtract), "hvr", 1)

        def apply(e, k):
            wait(e, "hrr", base["hrr"] + k + 1)
            inc(e.tensor_scalar(vbuf[:, k, :], xs[:, k, :],
                                stats[:, k, 2:3], stats[:, k, 4:5],
                                ALU.subtract, ALU.mult), "hap", 1)

        def act_hook(e, j):
            if j >= 1:
                sq(e, j - 1)
            if j >= 3:
                lnexp(e, j - 3)

        def dve_hook(e, j):
            e.tensor_scalar_mul(stats[:, j, 2:3], s1x[:, j:j + 1], INV_D)
            if j >= 1:
                e.tensor_mul(stats[:, j - 1, 7:8], stats[:, j - 1, 2:3], stats[:, j - 1, 2:3])
            if do_apply and j >= 4:
                apply(e, j - 4)
            if j >= 2:
                var(e, j - 2)

        def tail_act(e):
            sq(e, NT - 1)
            for k in range(NT - 3, NT):
                lnexp(e, k)

        def tail_vec(e):
            e.tensor_mul(stats[:, NT - 1, 7:8], stats[:, NT - 1, 2:3], stats[:, NT - 1, 2:3])
            for k in range(NT - 2, NT):
                var(e, k)
            if do_apply:
                for k in range(NT - 4, NT):
                    apply(e, k)

        return {"v2": True, "do_apply": do_apply, "dve": dve_hook, "act": act_hook,
                "base": base, "tail_act": tail_act, "tail_vec": tail_vec}

    def emit_preln(blk, prev=None, hooks=None):
        take_snap()
        v2 = hooks is not None and hooks.get("v2")
        if hooks is not None and not v2:
            b_hrr = hooks["base"]["hrr"]
        b_sq, b_vr, b_rr = snap["sq"], snap["vr"], snap["rr"]
        b_ap, b_tr, b_ev = snap["ap"], snap["tr"], snap["ev"]
        p_pl = snap["pl"] - NT
        p_oo = snap["oo"] - NT
        p_ze = snap["ze"] - NT

        if hooks is None:
            @blk.scalar
            def _(e):
                if prev is None:
                    barrier(e)
                for t in range(NT):
                    if prev is not None:
                        wait(e, "pl", p_pl + t + 1)
                    else:
                        wait(e, "lx", (t // 2 + 1) * 16)
                    inc(e.activation(xbt[:, 0, :], xs[:, t, :], AF.Square,
                                     accum_out=stats[:, t, 1:2]), "sq", 1)
                    wait(e, "vr", b_vr + t + 1)
                    e.activation(stats[:, t, 6:7], stats[:, t, 3:4], AF.Ln, bias=cst[:, 0:1])
                    inc(e.activation(stats[:, t, 4:5], stats[:, t, 6:7], AF.Exp, scale=-0.5), "rr", 1)

        @blk.vector
        def _(e):
            e = wrapv(e)
            if prev is None:
                barrier(e)
            if v2:
                hooks["tail_vec"](e)
                if not hooks["do_apply"]:
                    if prev == "l1a":
                        wait(e, "oo", p_oo + NT)
                    for t in range(NT):
                        wait(e, "hrr", hooks["base"]["hrr"] + t + 1)
                        inc(e.tensor_scalar(vbuf[:, t, :], xs[:, t, :],
                                            stats[:, t, 2:3], stats[:, t, 4:5],
                                            ALU.subtract, ALU.mult), "ap", 1)
                return
            if hooks is None:
                for t in range(NT):
                    if prev is None:
                        wait(e, "lx", (t // 2 + 1) * 16)
                    e.reduce_sum(s1x[:, t:t + 1], xs[:, t, :], axis=AX.X)
                    e.tensor_scalar_mul(stats[:, t, 2:3], s1x[:, t:t + 1], INV_D)
                    e.tensor_mul(stats[:, t, 7:8], stats[:, t, 2:3], stats[:, t, 2:3])
                    wait(e, "sq", b_sq + t + 1)
                    inc(e.tensor_scalar(stats[:, t, 3:4], stats[:, t, 1:2], INV_D,
                                        stats[:, t, 7:8], ALU.mult, ALU.subtract), "vr", 1)
            if prev == "l1a":
                wait(e, "oo", p_oo + NT)
            for t in range(NT):
                if prev in ("l0a", "ffn0"):
                    wait(e, "oo", p_oo + t + 1)
                    wait(e, "ze", p_ze + t + 1)
                wait(e, "rr" if hooks is None else "hrr",
                     (b_rr if hooks is None else b_hrr) + t + 1)
                inc(e.tensor_scalar(vbuf[:, t, :], xs[:, t, :],
                                    stats[:, t, 2:3], stats[:, t, 4:5],
                                    ALU.subtract, ALU.mult), "ap", 1)

        @blk.tensor
        def _(e):
            if prev is None:
                barrier(e)
            else:
                wait(e, "ev", snap["ev"])
                wait(e, "at", snap["at"])
                wait(e, "te", snap["te"])
                wait(e, "pp", snap["pp"])
            for t in range(NT):
                if v2 and hooks["do_apply"]:
                    wait(e, "hap", hooks["base"]["hap"] + t + 1)
                else:
                    wait(e, "ap", b_ap + t + 1)
                if t >= 2:
                    wait(e, "ev", b_ev + t - 1)
                ins = None
                for dc in range(DC):
                    ins = e.transpose(pT[:, t % 2, dc * 128:(dc + 1) * 128],
                                      vbuf[:, t, dc * 128:(dc + 1) * 128],
                                      ident[:, :])
                inc(ins, "tr", 1)

        @blk.scalar
        def _(e):
            for t in range(NT):
                if v2 and t == NT - 4:
                    hooks["tail_act"](e)
                wait(e, "tr", b_tr + t + 1)
                src_ = pT[:, t % 2, :].rearrange("p (dc q) -> p dc q", dc=DC)
                inc(e.activation(xbf[:, :, t * 128:(t + 1) * 128], src_, AF.Copy), "ev", 1)

    def emit_proj_kvq(blk, layer):
        l0 = (layer == 0)
        take_snap()
        b_kg, b_kev = snap["kg"], snap["kev"]
        b_vg, b_vev = snap["vg"], snap["vev"]
        b_qg, b_qev = snap["qg"], snap["qev"]

        b_evp = snap["ev"] - NT

        @blk.tensor
        def _(e):
            barrier(e, skip=("ev",))
            wait_mark(e, "wk0" if l0 else "wk1", ["wl"])
            for ec in range(DC):
                for g in range(4):
                    wait(e, "ev", b_evp + 4 * (g + 1))
                    if ec > 0:
                        wait(e, "kev", b_kev + (ec - 1) * 4 + g + 1)
                    ins = None
                    for dc in range(DC):
                        ins = e.matmul(pA[:, g * 512:(g + 1) * 512],
                                       w2[:, 0, dc * 1024 + ec * 128: dc * 1024 + (ec + 1) * 128],
                                       xbf[:, dc, g * 512:(g + 1) * 512],
                                       start=(dc == 0), stop=(dc == DC - 1))
                    inc(ins, "kg", 1)
            set_mark(f"l{layer}k")

        @blk.scalar
        def _(e):
            for ec in range(DC):
                for g in range(4):
                    wait(e, "kg", b_kg + ec * 4 + g + 1)
                    if l0:
                        ins = e.activation(kbuf[:, ec, g * 512:(g + 1) * 512],
                                           pA[:, g * 512:(g + 1) * 512], AF.Identity,
                                           bias=bkc[:, ec:ec + 1])
                    else:
                        ins = e.activation(kbuf[:, ec, g * 512:(g + 1) * 512],
                                           pA[:, g * 512:(g + 1) * 512], AF.Copy)
                    inc(ins, "kev", 1)

        @blk.gpsimd
        def _(e):
            wait_mark(e, f"l{layer}k", ["kg"])
            wq = dr["l0wq" if l0 else "l1wq"]
            inc(e.dma_start(w2[:, 0, :].rearrange("p (dc o) -> p dc o", dc=8), wq.rearrange("(dc p) o -> p dc o", p=128)), "wl", 16)
            set_mark(f"wq{layer}")

        @blk.tensor
        def _(e):
            wait_mark(e, "wv0" if l0 else "wv1", ["wl"])
            if l0:
                wait_mark(e, "bv", ["wl"])
            wait(e, "kev", b_kev + 32)
            for t in range(NT):
                wait(e, "ev", b_evp + t + 1)
                if t >= 2:
                    wait(e, "vev", b_vev + t - 1)
                po = (t % 2) * 1024
                ins = None
                for og in range(2):
                    for dc in range(DC):
                        ins = e.matmul(pA[:, po + og * 512: po + (og + 1) * 512],
                                       xbf[:, dc, t * 128:(t + 1) * 128],
                                       w2[:, 1, dc * 1024 + og * 512: dc * 1024 + (og + 1) * 512],
                                       start=(dc == 0), stop=(dc == DC - 1 and not l0))
                    if l0:
                        ins = e.matmul(pA[:, po + og * 512: po + (og + 1) * 512],
                                       ones_row[:, :], brow[:, og * 512:(og + 1) * 512],
                                       start=False, stop=True)
                inc(ins, "vg", 1)
            set_mark(f"l{layer}v")

        @blk.scalar
        def _(e):
            for t in range(NT):
                wait(e, "vg", b_vg + t + 1)
                po = (t % 2) * 1024
                inc(e.activation(vbuf[:, t, :], pA[:, po:po + 1024], AF.Copy), "vev", 1)

        @blk.gpsimd
        def _(e):
            wait_mark(e, f"l{layer}v", ["vg"])
            if l0:
                inc(e.dma_start(brow, dr["l0bb"][:, :]), "wl", 16)
                set_mark("bb")
                inc(e.dma_start(w2[:, 1, :].rearrange("p (dc o) -> p dc o", dc=8), dr["l0bw"].rearrange("(dc p) o -> p dc o", p=128)), "wl", 16)
                set_mark("bw")
            else:
                inc(e.dma_start(w2[:, 1, 0:1024].rearrange("p (dc o) -> p dc o", dc=8), dr["l1rw"].rearrange("(dc p) o -> p dc o", p=128)), "wl", 16)
                set_mark("wrw")

        @blk.tensor
        def _(e):
            wait_mark(e, f"wq{layer}", ["wl"])
            wait(e, "vev", b_vev + NT)
            for g8 in range(8):
                if g8 >= 1:
                    wait(e, "qev", b_qev + g8)
                ins = None
                for ec in range(DC):
                    for dc in range(DC):
                        ins = e.matmul(pA[:, ec * 256:(ec + 1) * 256],
                                       w2[:, 0, dc * 1024 + ec * 128: dc * 1024 + (ec + 1) * 128],
                                       xbf[:, dc, g8 * 256:(g8 + 1) * 256],
                                       start=(dc == 0), stop=(dc == DC - 1))
                inc(ins, "qg", 1)
            set_mark(f"l{layer}q")

        @blk.scalar
        def _(e):
            for g8 in range(8):
                wait(e, "qg", b_qg + g8 + 1)
                ins = None
                for ec in range(DC):
                    if l0:
                        ins = e.activation(xbf[:, ec, g8 * 256:(g8 + 1) * 256],
                                           pA[:, ec * 256:(ec + 1) * 256], AF.Identity,
                                           bias=bqc[:, ec:ec + 1])
                    else:
                        ins = e.activation(xbf[:, ec, g8 * 256:(g8 + 1) * 256],
                                           pA[:, ec * 256:(ec + 1) * 256], AF.Copy,
                                           scale=1.0 / 32.0)
                inc(ins, "qev", 1)

        @blk.gpsimd
        def _(e):
            if l0:
                wait_mark(e, "l0q", ["qg"])
                inc(e.dma_start(wlb, dr["l0lb"][:, :]), "wl", 16)
                set_mark("wlb")
            if not l0:
                wait_mark(e, "l1q", ["qg"])
                for n in range(2):
                    inc(e.dma_start(w2[:, 0, n * 4096:(n + 1) * 4096].rearrange("p (dc h) -> p dc h", dc=8),
                                    dr["l1we"][n].rearrange("(dc p) h -> p dc h", p=128)), "wl", 16)
                    set_mark(f"we{n}")

    def emit_build_mask(blk):
        take_snap()

        @blk.vector
        def _(e):
            e = wrapv(e)
            barrier(e)
            e.memset(probs, 0.0)
            inc(e.memset(mask_rel, 1.0), "dv", 1)

        @blk.gpsimd
        def _(e):
            wait_mark(e, "l0f_mm", ["oo"])
            wait(e, "dv", cnt["dv"])
            e.affine_select(mask_rel, mask_rel, [[1, 640]], ALU.is_ge, 0.0,
                            base=0, channel_multiplier=-1)
            inc(e.affine_select(mask_rel, mask_rel, [[-1, 640]], ALU.is_ge, 0.0,
                                base=511, channel_multiplier=1), "gp", 1)

    def emit_attn_core(blk, layer, trunc=6, first_attn=True, hooks=None):
        l0 = (layer == 0)
        nk = 1 if l0 else NT
        scw = 128 if l0 else 2048
        take_snap()
        b_sc, b_mx, b_ex = snap["sc"], snap["mx"], snap["ex"]
        b_tr, b_at, b_oo = snap["tr"], snap["at"], snap["oo"]
        b_pl, b_s2, b_vr2, b_rr = snap["pl"], snap["s2p"], snap["vr2"], snap["rr"]
        b_ze, b_fl = snap["ze"], snap["fl"]

        sc_ap = lambda t: pA[:, (t % 8) * 128:(t % 8) * 128 + 128]
        out_ps = (lambda j: pB[:, 0:1024] if j % 2 == 0 else pA[:, 1024:2048]) if l0 \
            else (lambda j: pB[:, 0:1024])
        pr_ap = (lambda t: probs[:, t * 128:(t + 1) * 128]) if l0 \
            else (lambda t: probs[:, 0:scw])
        atT = (lambda t, i: attnT(t)) if l0 else (lambda t, i: attnT(i))

        def emit_out(e, j):
            if l0:
                if j >= 2:
                    wait(e, "ze", b_ze + j - 1)
            else:
                wait(e, "pl", b_pl + j)
            bp = out_ps(j)
            ins = None
            for og in range(2):
                for i in range(nk):
                    kt = j if l0 else i
                    if og == 0:
                        wait(e, "at", b_at + j * nk + i + 1)
                    ins = e.matmul(bp[:, og * 512:(og + 1) * 512],
                                   atT(j, i),
                                   vbuf[:, kt, og * 512:(og + 1) * 512],
                                   start=(i == 0), stop=(i == nk - 1))
            inc(ins, "oo", 1)

        @blk.tensor
        def _(e):
            barrier(e)
            n_tr = 0
            if l0:
                for g in range(4):
                    if g >= 2:
                        wait(e, "ex", b_ex + (g - 1) * 4)
                    for t in range(g * 4, g * 4 + 4):
                        ins = None
                        for ec in range(DC):
                            ins = e.matmul(sc_ap(t),
                                           xbf[:, ec, t * 128:(t + 1) * 128],
                                           kbuf[:, ec, t * 128:(t + 1) * 128],
                                           start=(ec == 0), stop=(ec == DC - 1))
                        inc(ins, "sc", 1)
                for t in range(NT):
                    if t >= 1:
                        emit_out(e, t - 1)
                    wait(e, "ex", b_ex + t + 1)
                    if n_tr >= 2:
                        wait(e, "at", b_at + n_tr - 1)
                    inc(e.transpose(pT[:, n_tr % 2, 0:128],
                                    pr_ap(t), ident[:, :]), "tr", 1)
                    n_tr += 1
                emit_out(e, NT - 1)
                set_mark(f"l{layer}att")
                return
            for t in range(NT):
                if t >= 1:
                    wait(e, "ex", b_ex + t)
                lo = max(0, 128 * t - 256)
                hi = min(2048, 128 * t + 384)
                ins = None
                for kg in range(lo // 512, (hi + 511) // 512):
                    for ec in range(DC):
                        ins = e.matmul(pA[:, kg * 512:(kg + 1) * 512],
                                       xbf[:, ec, t * 128:(t + 1) * 128],
                                       kbuf[:, ec, kg * 512:(kg + 1) * 512],
                                       start=(ec == 0), stop=(ec == DC - 1))
                inc(ins, "sc", 1)
                if t >= 1:
                    emit_out(e, t - 1)
                wait(e, "fl", b_fl + t + 1)
                for i in range(nk):
                    if n_tr >= 2:
                        wait(e, "at", b_at + n_tr - 1)
                    ins = e.transpose(pT[:, n_tr % 2, 0:128],
                                      probs[:, i * 128:(i + 1) * 128],
                                      ident[:, :])
                    inc(ins, "tr", 1)
                    n_tr += 1
            emit_out(e, NT - 1)
            set_mark(f"l{layer}att")

        @blk.vector
        def _(e):
            e = wrapv(e)
            barrier(e)
            n_tr = 0

            def pl_front(j):
                bp = vbuf[:, j, :] if l0 else out_ps(j)
                wait(e, "oo", b_oo + j + 1)
                if l0:
                    wait(e, "ze", b_ze + j + 1)
                    e.scalar_tensor_tensor(xbt[:, 0, :], bp, 1.0, bp,
                                           ALU.mult, ALU.mult, accum_out=stats[:, j, 1:2])
                else:
                    wait(e, "s2p", b_s2 + j + 1)
                rinv = stats[:, j, 7:8]
                e.tensor_scalar(stats[:, j, 2:3], stats[:, j, 0:1], rinv, INV_D,
                                ALU.mult, ALU.mult)
                e.tensor_scalar(stats[:, j, 3:4], stats[:, j, 1:2], rinv, rinv,
                                ALU.mult, ALU.mult)
                e.tensor_mul(stats[:, j, 1:2], stats[:, j, 2:3], stats[:, j, 2:3])
                inc(e.tensor_scalar(stats[:, j, 3:4], stats[:, j, 3:4], INV_D,
                                    stats[:, j, 1:2], ALU.mult, ALU.subtract), "vr2", 1)

            def pl_back(j):
                bp = vbuf[:, j, :] if l0 else out_ps(j)
                rinv = stats[:, j, 7:8]
                wait(e, "rr", b_rr + j + 1)
                e.tensor_mul(stats[:, j, 5:6], stats[:, j, 2:3], stats[:, j, 4:5])
                e.tensor_mul(stats[:, j, 1:2], stats[:, j, 4:5], rinv)
                if layer == 0 and first_attn:
                    e.tensor_copy(macc[:, j:j + 1], stats[:, j, 5:6])
                else:
                    e.tensor_add(macc[:, j:j + 1], macc[:, j:j + 1], stats[:, j, 5:6])
                e.scalar_tensor_tensor(s1x[:, j:j + 1], stats[:, j, 5:6], float(D),
                                       s1x[:, j:j + 1], ALU.mult, ALU.add)
                inc(e.scalar_tensor_tensor(xs[:, j, :], bp, stats[:, j, 1:2],
                                           xs[:, j, :], ALU.mult, ALU.add), "pl", 1)
                if hooks is not None:
                    hooks["dve"](e, j)

            def postln(j):
                pl_front(j)
                pl_back(j)

            if l0:
                for g in range(4):
                    wait(e, "sc", b_sc + (g + 1) * 4)
                    if g >= 2:
                        wait(e, "ex", b_ex + (g - 1) * 4)
                    for t in range(g * 4, g * 4 + 4):
                        inc(e.reduce_max(stats[:, t, 6:7], sc_ap(t), axis=AX.X,
                                         negate=True), "mx", 1)
                for t in range(NT):
                    wait(e, "ex", b_ex + t + 1)
                    e.reduce_sum(stats[:, t, 6:7], pr_ap(t), axis=AX.X)
                    e.reciprocal(stats[:, t, 7:8], stats[:, t, 6:7])
                    if t >= 1:
                        pl_front(t - 1)
                    if t >= 2:
                        pl_back(t - 2)
                pl_front(NT - 1)
                pl_back(NT - 2)
                pl_back(NT - 1)
            else:
                for t in range(NT):
                    wait(e, "sc", b_sc + t + 1)
                    lo = max(0, 128 * t - 256)
                    hi = min(2048, 128 * t + 384)
                    off = lo - (128 * t - 256)
                    e.tensor_tensor(pA[:, lo:hi], pA[:, lo:hi],
                                    mask_rel[:, off:off + (hi - lo)], ALU.mult)
                    e.reduce_max(stats[:, t, 6:7], pA[:, lo:hi], axis=AX.X, negate=True)
                    inc(e.tensor_single_scalar(stats[:, t, 6:7], stats[:, t, 6:7],
                                               0.0, ALU.min), "mx", 1)
                    wait(e, "ex", b_ex + t + 1)
                    ins = None
                    if lo > 0:
                        ins = e.tensor_scalar(probs[:, 0:lo], probs[:, 0:lo], 0.0,
                                              stats[:, t, 5:6], ALU.mult, ALU.add)
                    if hi < 2048:
                        ins = e.tensor_scalar(probs[:, hi:2048], probs[:, hi:2048], 0.0,
                                              stats[:, t, 5:6], ALU.mult, ALU.add)
                    inc(ins, "fl", 1)
                    e.reduce_sum(stats[:, t, 0:1], probs[:, lo:hi], axis=AX.X)
                    e.scalar_tensor_tensor(stats[:, t, 0:1], stats[:, t, 5:6],
                                           float(2048 - (hi - lo)),
                                           stats[:, t, 0:1], ALU.mult, ALU.add)
                    e.reciprocal(stats[:, t, 7:8], stats[:, t, 0:1])
                    if t >= 1:
                        postln(t - 1)
                postln(NT - 1)

        @blk.scalar
        def _(e):
            def postact(j):
                wait(e, "oo", b_oo + j + 1)
                if l0:
                    inc(e.activation(vbuf[:, j, :], out_ps(j), AF.Identity,
                                     accum_out=stats[:, j, 0:1]), "ze", 1)
                else:
                    e.activation(xbt[:, 0, :], out_ps(j), AF.Identity,
                                 accum_out=stats[:, j, 0:1])
                    inc(e.activation(xbt[:, 0, :], out_ps(j), AF.Square,
                                     accum_out=stats[:, j, 1:2]), "s2p", 1)
                wait(e, "vr2", b_vr2 + j + 1)
                e.activation(stats[:, j, 6:7], stats[:, j, 3:4], AF.Ln, bias=cst[:, 0:1])
                inc(e.activation(stats[:, j, 4:5], stats[:, j, 6:7], AF.Exp, scale=-0.5), "rr", 1)
                if hooks is not None:
                    hooks["act"](e, j)

            n_tr = 0
            if l0:
                for g in range(4):
                    wait(e, "mx", b_mx + (g + 1) * 4)
                    for t in range(g * 4, g * 4 + 4):
                        inc(e.activation(pr_ap(t), sc_ap(t), AF.Exp,
                                         bias=stats[:, t, 6:7]), "ex", 1)
                    if g >= 1:
                        for j in range((g - 1) * 4, g * 4):
                            wait(e, "tr", b_tr + j + 1)
                            inc(e.activation(atT(j, 0), pT[:, j % 2, 0:128], AF.Copy), "at", 1)
                            if j >= 2:
                                postact(j - 2)
                for j in range(12, 16):
                    wait(e, "tr", b_tr + j + 1)
                    inc(e.activation(atT(j, 0), pT[:, j % 2, 0:128], AF.Copy), "at", 1)
                    postact(j - 2)
                postact(14)
                postact(15)
            else:
                for t in range(NT + 1):
                    if t < NT:
                        lo = max(0, 128 * t - 256)
                        hi = min(2048, 128 * t + 384)
                        wait(e, "mx", b_mx + t + 1)
                        e.activation(probs[:, lo:hi], pA[:, lo:hi], AF.Exp,
                                     bias=stats[:, t, 6:7])
                        inc(e.activation(stats[:, t, 5:6], stats[:, t, 6:7], AF.Exp), "ex", 1)
                        for i in range(nk):
                            wait(e, "tr", b_tr + n_tr + 1)
                            inc(e.activation(atT(t, i), pT[:, n_tr % 2, 0:128], AF.Copy), "at", 1)
                            n_tr += 1
                    if t >= 1:
                        postact(t - 1)

    def emit_ffn0(blk, hooks=None):
        take_snap()
        b_sm, b_mx, b_fg = snap["sm"], snap["mx"], snap["fg"]
        b_tt, b_tp, b_te = snap["tt"], snap["tp"], snap["te"]
        b_oo, b_pl, b_s2f, b_vr2, b_rr = snap["oo"], snap["pl"], snap["s2p"], snap["vr2"], snap["rr"]
        b_ze = snap["ze"]

        base_ps = lambda t: pB[:, 0:1024] if t % 2 == 0 else pA[:, 1024:2048]
        sm_ap = lambda t: pA[:, (t // 8) * 512 + (t % 8) * 16:(t // 8) * 512 + (t % 8) * 16 + 16]
        t64T_ap = lambda t: w2[0:64, 0, 1024 + t * 128: 1152 + t * 128]
        t64_ap = lambda t: w2[:, 0, 3072 + t * 64: 3136 + t * 64]

        @blk.tensor
        def _(e):
            wait(e, "ex", snap["ex"])
            wait(e, "mx", snap["mx"])
            wait_mark(e, "bw", ["wl"])
            wait_mark(e, "bb", ["wl"])
            wait_mark(e, "wlb", ["wl"])
            b_evp = snap["ev"] - NT
            for t in range(NT):
                wait(e, "ev", b_evp + NT - 16 + t + 1)
                ins = None
                for dc in range(DC):
                    ins = e.matmul(sm_ap(t), xbf[:, dc, t * 128:(t + 1) * 128],
                                   wsm[:, dc, :], start=(dc == 0), stop=False)
                ins = e.matmul(sm_ap(t), ones_row[:, :], rb_row[:, :], start=False, stop=True)
                inc(ins, "sm", 1)
            wait(e, "ev", snap["ev"])
            wait(e, "at", snap["at"])
            for t in range(NT):
                wait(e, "tt", b_tt + t + 1)
                if t >= 2:
                    wait(e, "te", b_te + t - 1)
                inc(e.transpose(pT[0:64, t % 2, 0:128], t64_ap(t), ident[:, :]), "tp", 1)
            wait(e, "ze", b_ze)
            for t in range(NT):
                if t >= 2:
                    wait(e, "ze", b_ze + t - 1)
                bp = base_ps(t)
                for og in range(2):
                    for dc in range(DC):
                        e.matmul(bp[:, og * 512:(og + 1) * 512],
                                 xbf[:, dc, t * 128:(t + 1) * 128],
                                 w2[:, 1, dc * 1024 + og * 512: dc * 1024 + (og + 1) * 512],
                                 start=(dc == 0), stop=False)
                    e.matmul(bp[:, og * 512:(og + 1) * 512], ones_row[:, :],
                             brow[:, og * 512:(og + 1) * 512], start=False, stop=False)
                wait(e, "te", b_te + t + 1)
                ins = None
                for og in range(2):
                    ins = e.matmul(bp[:, og * 512:(og + 1) * 512], t64T_ap(t),
                                   wlb[:, og * 512:(og + 1) * 512], start=False, stop=True)
                inc(ins, "oo", 1)
            set_mark("l0f_mm")

        @blk.vector
        def _(e):
            e = wrapv(e)
            barrier(e)
            gall_s = lambda t: u8[:, 1024:2048].bitcast(F32)[:, t * 16:t * 16 + 8]
            g2f_s = lambda t: u8[:, 1024:2048].bitcast(F32)[:, t * 16 + 8:t * 16 + 16]
            gbf_s = lambda t: u8[:, 2048 + 8 * t:2048 + 8 * t + 8]
            for t in range(NT):
                wait(e, "sm", b_sm + (8 if t < 8 else 16))
                inc(e.reduce_max(stats[:, t, 6:7], sm_ap(t)[:, 0:8], axis=AX.X, negate=True), "mx", 1)
            for t in range(NT):
                wait(e, "fg", b_fg + t + 1)
                e.reduce_sum(stats[:, t, 0:1], gall_s(t), axis=AX.X)
            for t in range(NT):
                e.reciprocal(stats[:, t, 7:8], stats[:, t, 0:1])
            for t in range(NT):
                e.tensor_scalar_mul(gbf_s(t), gall_s(t), stats[:, t, 7:8])
            for t in range(NT):
                e.tensor_scalar_mul(g2f_s(t), gbf_s(t), 2.0)
            for t in range(NT):
                ins = None
                mid = sm_ap(t)[:, 8:16]
                for ex in range(8):
                    ins = e.tensor_single_scalar(t64_ap(t)[:, ex * 8:(ex + 1) * 8], mid,
                                                 g2f_s(t)[:, ex:ex + 1], ALU.mult)
                inc(ins, "tt", 1)

            def postln(j):
                bp = vbuf[:, j, :]
                wait(e, "ze", b_ze + j + 1)
                wait(e, "s2p", b_s2f + j + 1)
                e.tensor_scalar_mul(stats[:, j, 2:3], stats[:, j, 0:1], INV_D)
                e.tensor_mul(stats[:, j, 6:7], stats[:, j, 2:3], stats[:, j, 2:3])
                inc(e.tensor_scalar(stats[:, j, 3:4], stats[:, j, 1:2], INV_D,
                                    stats[:, j, 6:7], ALU.mult, ALU.subtract), "vr2", 1)
                wait(e, "rr", b_rr + j + 1)
                e.tensor_mul(stats[:, j, 5:6], stats[:, j, 2:3], stats[:, j, 4:5])
                e.tensor_add(macc[:, j:j + 1], macc[:, j:j + 1], stats[:, j, 5:6])
                e.scalar_tensor_tensor(s1x[:, j:j + 1], stats[:, j, 5:6], float(D),
                                       s1x[:, j:j + 1], ALU.mult, ALU.add)
                inc(e.scalar_tensor_tensor(xs[:, j, :], bp, stats[:, j, 4:5],
                                           xs[:, j, :], ALU.mult, ALU.add), "pl", 1)
                if hooks is not None:
                    hooks["dve"](e, j)

            for j in range(NT):
                postln(j)

        @blk.scalar
        def _(e):
            gall_s = lambda t: u8[:, 1024:2048].bitcast(F32)[:, t * 16:t * 16 + 8]
            for t in range(NT):
                wait(e, "mx", b_mx + t + 1)
                inc(e.activation(gall_s(t), sm_ap(t)[:, 0:8], AF.Exp, bias=stats[:, t, 6:7]), "fg", 1)
            for t in range(NT):
                wait(e, "tp", b_tp + t + 1)
                inc(e.activation(t64T_ap(t), pT[0:64, t % 2, 0:128], AF.Copy), "te", 1)

            def postact(j):
                wait(e, "oo", b_oo + j + 1)
                inc(e.activation(vbuf[:, j, :], base_ps(j), AF.Identity,
                                 accum_out=stats[:, j, 0:1]), "ze", 1)
                inc(e.activation(xbt[:, 0, :], vbuf[:, j, :], AF.Square,
                                 accum_out=stats[:, j, 1:2]), "s2p", 1)
                wait(e, "vr2", b_vr2 + j + 1)
                e.activation(stats[:, j, 6:7], stats[:, j, 3:4], AF.Ln, bias=cst[:, 0:1])
                inc(e.activation(stats[:, j, 4:5], stats[:, j, 6:7], AF.Exp, scale=-0.5), "rr", 1)
                if hooks is not None:
                    hooks["act"](e, j)

            for j in range(NT):
                postact(j)

        @blk.gpsimd
        def _(e):
            wait_mark(e, "l0f_mm", ["oo"])
            inc(e.dma_start(w2[:, 0, :].rearrange("p (dc o) -> p dc o", dc=8), dr["l1wk"].rearrange("(dc p) o -> p dc o", p=128)), "wl", 16)
            set_mark("wk1")
            inc(e.dma_start(w2[:, 1, :].rearrange("p (dc o) -> p dc o", dc=8), dr["l1wv"].rearrange("(dc p) o -> p dc o", p=128)), "wl", 16)
            set_mark("wv1")

    def emit_ffn1_prep(blk):
        take_snap()
        b_cn, b_cb, b_tp, b_k1 = snap["cn"], snap["cb"], snap["tp"], snap["k1"]

        @blk.scalar
        def _(e):
            wait(e, "ld", 32)
            e.activation(et_bf[:, :], tmp_r[:, :], AF.Exp)
            e.activation(xbt[0:16, 0, 0:128], cent[:, :], AF.Square, accum_out=csc[:, 0:1])
            e.activation(csc[:, 1:2], csc[:, 0:1], AF.Ln, bias=cst[0:16, 1:2])
            inc(e.activation(csc[:, 0:1], csc[:, 1:2], AF.Exp, scale=-0.5), "cn", 1)

        @blk.vector
        def _(e):
            e = wrapv(e)
            wait(e, "cn", b_cn + 1)
            inc(e.tensor_scalar_mul(cn_bf, cent[:, :], csc[:, 0:1]), "cb", 1)

        @blk.tensor
        def _(e):
            wait(e, "cb", b_cb + 1)
            inc(e.transpose(pT[:, 0, 0:16], cn_bf, ident[0:16, 0:16]), "tp", 1)
            inc(e.matmul(pB[:, 0:1], ones_row[:, :], et_bf[:, :], start=True, stop=True), "k1", 1)

        @blk.vector
        def _(e):
            e = wrapv(e)
            wait(e, "tp", b_tp + 1)
            e.tensor_copy(cnT[:, :], pT[:, 0, 0:16])
            wait(e, "k1", b_k1 + 1)
            inc(e.tensor_copy(etb[:, :], pB[:, 0:1]), "pp", 1)

    def emit_ffn1(blk):
        take_snap()
        b_zg, b_zev = snap["zg"], snap["zev"]
        b_sm, b_rz, b_v3, b_fg, b_wd = snap["sm"], snap["rz"], snap["v3"], snap["fg"], snap["wd"]
        b_hg, b_ge, b_og, b_zz, b_ew = snap["hg"], snap["ge"], snap["og"], snap["zz"], snap["ew"]
        b_s2, b_vr2, b_rr = snap["s2p"], snap["vr2"], snap["rr"]

        b_evp = snap["ev"] - NT

        @blk.tensor
        def _(e):
            wait_mark(e, "wrw", ["wl"])
            wait(e, "pp", snap["pp"])
            ins = None
            for kg in range(4):
                wait(e, "ev", b_evp + 4 * (kg + 1))
                for dc in range(DC):
                    ins = e.matmul(pA[:, kg * 512:(kg + 1) * 512],
                                   w2[:, 1, dc * 128:(dc + 1) * 128],
                                   xbf[:, dc, kg * 512:(kg + 1) * 512],
                                   start=(dc == 0), stop=(dc == DC - 1))
            inc(ins, "zg", 1)

        @blk.scalar
        def _(e):
            wait(e, "zg", b_zg + 1)
            e.activation(z_bf, pA[:, :], AF.Copy)
            inc(e.activation(z2_bf, pA[:, :], AF.Square), "zev", 1)

        @blk.tensor
        def _(e):
            wait(e, "zev", b_zev + 1)
            for t in range(NT):
                po = t * 32
                e.matmul(pB[:, po:po + 1], z2_bf[:, t * 128:(t + 1) * 128], onescol[:, :],
                         start=True, stop=True)
                inc(e.matmul(pB[:, po + 1:po + 17], z_bf[:, t * 128:(t + 1) * 128], cnT[:, :],
                             start=True, stop=True), "sm", 1)

        @blk.scalar
        def _(e):
            wait(e, "sm", b_sm + NT)
            e.activation(rstdA.rearrange("p (t c) -> p t c", c=1),
                         pB[:, 0:512].rearrange("p (t c) -> p t c", c=32)[:, :, 0:1],
                         AF.Ln, bias=cst[:, 1:2])
            inc(e.activation(rstd16, rstdA, AF.Exp, scale=-0.5), "rz", 1)

        @blk.vector
        def _(e):
            e = wrapv(e)
            wait(e, "rz", b_rz + 1)
            e.tensor_scalar(rscl16, rstd16, etb[:, :], ISQ128, ALU.mult, ALU.mult)
            for t in range(NT):
                wait(e, "sm", b_sm + t + 1)
                inc(e.tensor_scalar_mul(lsc_all[:, t * 16:(t + 1) * 16],
                                        pB[:, t * 32 + 1:t * 32 + 17],
                                        rscl16[:, t:t + 1]), "v3", 1)
            wait(e, "fg", b_fg + 1)
            for t in range(NT):
                e.reduce_sum(rsum16[:, t:t + 1], s16all[:, t * 16:(t + 1) * 16], axis=AX.X)
            e.reciprocal(rsum16[:, :], rsum16[:, :])
            for t in range(NT):
                e.reduce_max(m1_16[:, t:t + 1], s16all[:, t * 16:(t + 1) * 16], axis=AX.X)
            for t in range(NT):
                e.tensor_single_scalar(exm[:, t * 16:(t + 1) * 16],
                                       s16all[:, t * 16:(t + 1) * 16],
                                       m1_16[:, t:t + 1], ALU.is_lt)
            for t in range(NT):
                e.tensor_mul(exm[:, t * 16:(t + 1) * 16], exm[:, t * 16:(t + 1) * 16],
                             s16all[:, t * 16:(t + 1) * 16])
            for t in range(NT):
                e.reduce_max(m1_16[:, t:t + 1], exm[:, t * 16:(t + 1) * 16], axis=AX.X)
            for t in range(NT):
                e.tensor_single_scalar(exm[:, t * 16:(t + 1) * 16],
                                       s16all[:, t * 16:(t + 1) * 16],
                                       m1_16[:, t:t + 1], ALU.is_ge)
            for t in range(NT):
                inc(e.scalar_tensor_tensor(w_all[:, t, :], exm[:, t * 16:(t + 1) * 16],
                                           rsum16[:, t:t + 1],
                                           s16all[:, t * 16:(t + 1) * 16],
                                           ALU.mult, ALU.mult), "wd", 1)

        @blk.gpsimd
        def _(e):
            wait_mark(e, "l1att", ["oo"])
            inc(e.dma_start(wout[:, :, :], dr["l1wo"][0].rearrange("(hc p) o -> p hc o", p=128)), "wl", 16)
            set_mark("wo0")
            for n in range(1, 16):
                if n == 2:
                    wait(e, "sm", b_sm + NT)
                elif n >= 3:
                    wait(e, "hg", b_hg + (n - 2) * 16)
                if n >= 2:
                    inc(e.dma_start(we_dst(n),
                                    dr["l1we"][n].rearrange("(dc p) h -> p dc h", p=128)), "wl", 16)
                set_mark(f"we{n}x")
                wait(e, "ew", b_ew + n)
                inc(e.dma_start(wout[:, :, :],
                                dr["l1wo"][n].rearrange("(hc p) o -> p hc o", p=128)), "wl", 16)
                set_mark(f"wld{n}")

        @blk.tensor
        def _(e):
            wait(e, "zev", b_zev + 1)
            for n in range(16):
                if n == 0:
                    wait_mark(e, "we0", ["wl"])
                elif n == 1:
                    wait_mark(e, "we1", ["wl"])
                else:
                    wait_mark(e, f"we{n}x", ["wl"])
                for hc in range(4):
                    for g in range(4):
                        idx = n * 16 + hc * 4 + g
                        if n == 0 and hc == 0:
                            wait(e, "ev", b_evp + 4 * (g + 1))
                        if idx >= 4:
                            wait(e, "ge", b_ge + idx - 3)
                        ins = None
                        for dc in range(DC):
                            ins = e.matmul(pA[:, g * 512:(g + 1) * 512],
                                           we_ap(n, dc * 512 + hc * 128, dc * 512 + (hc + 1) * 128),
                                           xbf[:, dc, g * 512:(g + 1) * 512],
                                           start=(dc == 0), stop=(dc == DC - 1))
                        inc(ins, "hg", 1)
                wait_mark(e, "wo0" if n == 0 else f"wld{n}", ["wl"])
                if n == 0:
                    wait(e, "wd", b_wd + NT)
                for t in range(NT):
                    idx = n * 16 + t
                    if idx >= 2:
                        wait(e, "zz", b_zz + idx - 1)
                    po = (t % 2) * 512
                    ins = None
                    for hc in range(4):
                        if t == 0:
                            wait(e, "ge", b_ge + n * 16 + hc * 4 + 4)
                        ins = e.matmul(pB[:, po:po + 64],
                                       w2[:, 1, hc * 2048 + t * 128: hc * 2048 + (t + 1) * 128],
                                       wout[:, hc, :],
                                       start=(hc == 0), stop=(hc == 3))
                    inc(ins, "og", 1)
                inc(e.nop(), "ew", 1)

        @blk.scalar
        def _(e):
            for n in range(16):
                for hc in range(4):
                    for g in range(4):
                        idx = n * 16 + hc * 4 + g
                        if idx == 4:
                            wait(e, "v3", b_v3 + NT)
                            inc(e.activation(s16all, lsc_all, AF.Exp), "fg", 1)
                        wait(e, "hg", b_hg + idx + 1)
                        inc(e.activation(w2[:, 1, hc * 2048 + g * 512: hc * 2048 + (g + 1) * 512],
                                         pA[:, g * 512:(g + 1) * 512], AF.Gelu), "ge", 1)

        @blk.vector
        def _(e):
            e = wrapv(e)
            for n in range(16):
                for t in range(NT):
                    idx = n * 16 + t
                    wait(e, "og", b_og + idx + 1)
                    po = (t % 2) * 512
                    inc(e.tensor_scalar_mul(z_slot(t, n * 64, (n + 1) * 64),
                                            pB[:, po:po + 64], w_all[:, t, n:n + 1]), "zz", 1)

        b_zea, b_vr = snap["ze"], snap["vr"]

        def z_lnexp(e, k):
            wait(e, "vr2", b_vr2 + k + 1)
            e.activation(stats[:, k, 6:7], stats[:, k, 3:4], AF.Ln, bias=cst[:, 0:1])
            inc(e.activation(stats[:, k, 4:5], stats[:, k, 6:7], AF.Exp, scale=-0.5), "rr", 1)

        def z_apply(e, k):
            wait(e, "vr", b_vr + k + 1)
            inc(e.activation(vbuf[:, k % 4, :], z_slot(k, 0, 1024), AF.Identity,
                             bias=stats[:, k, 5:6], scale=stats[:, k, 4:5]), "ze", 1)

        @blk.scalar
        def _(e):
            for t in range(NT):
                wait(e, "zz", b_zz + 15 * NT + t + 1)
                inc(e.activation(xbt[:, 0, :], z_slot(t, 0, 1024), AF.Square,
                                 accum_out=stats[:, t, 1:2]), "s2p", 1)
                if t >= 1:
                    z_lnexp(e, t - 1)
                if t >= 2:
                    z_apply(e, t - 2)
            z_lnexp(e, NT - 1)
            z_apply(e, NT - 2)
            z_apply(e, NT - 1)

        @blk.vector
        def _(e):
            e = wrapv(e)

            def z_mr(e, k):
                wait(e, "rr", b_rr + k + 1)
                e.tensor_mul(stats[:, k, 5:6], stats[:, k, 2:3], stats[:, k, 4:5])

            def z_bias(e, k):
                inc(e.tensor_scalar(stats[:, k, 5:6], stats[:, k, 5:6], macc[:, k:k + 1],
                                    -1.0, ALU.add, ALU.mult), "vr", 1)

            def z_add(e, k):
                wait(e, "ze", b_zea + k + 1)
                inc(e.tensor_tensor(xs[:, k, :], xs[:, k, :], vbuf[:, k % 4, :],
                                    ALU.add), "pl", 1)

            for t in range(NT):
                wait(e, "zz", b_zz + 15 * NT + t + 1)
                e.reduce_sum(stats[:, t, 0:1], z_slot(t, 0, 1024), axis=AX.X)
                if t >= 1:
                    z_mr(e, t - 1)
                e.tensor_scalar_mul(stats[:, t, 2:3], stats[:, t, 0:1], INV_D)
                if t >= 1:
                    z_bias(e, t - 1)
                e.tensor_mul(stats[:, t, 6:7], stats[:, t, 2:3], stats[:, t, 2:3])
                if t >= 2:
                    z_add(e, t - 2)
                wait(e, "s2p", b_s2 + t + 1)
                inc(e.tensor_scalar(stats[:, t, 3:4], stats[:, t, 1:2], INV_D,
                                    stats[:, t, 6:7], ALU.mult, ALU.subtract), "vr2", 1)
            z_mr(e, NT - 1)
            z_add(e, NT - 2)
            z_bias(e, NT - 1)
            z_add(e, NT - 1)

    hooks_next = None
    with nc.Block() as blk:
        emit_setup(blk)
        for part in parts:
            if part == 'pre':
                emit_preln(blk)
            elif part == 'prekv':
                emit_preln(blk)
                emit_proj_kvq(blk, 0)
            elif part == 0:
                emit_preln(blk, prev=None)
                emit_proj_kvq(blk, 0)
                hk = make_preln_hooks_v2() if 1 in parts else None
                emit_attn_core(blk, 0, first_attn=True, hooks=hk)
                hooks_next = hk
            elif part == 1:
                emit_ffn1_prep(blk)
                emit_preln(blk, prev="l0a", hooks=hooks_next)
                hk = make_preln_hooks_v2() if 2 in parts else None
                emit_ffn0(blk, hooks=hk)
                hooks_next = hk
            elif part == 2:
                emit_preln(blk, prev="ffn0", hooks=hooks_next)
                emit_build_mask(blk)
                emit_proj_kvq(blk, 1)
                hk = make_preln_hooks() if 3 in parts else None
                emit_attn_core(blk, 1, first_attn=False, hooks=hk)
                hooks_next = hk
            elif part == 3:
                emit_preln(blk, prev="l1a", hooks=hooks_next)
                emit_ffn1(blk)

        take_snap()

        tap_map = {"xbf": xbf, "kbuf": kbuf, "vbuf": vbuf, "u8": u8, "stats": stats,
                   "w2": w2, "m640": m640, "xbt": xbt}

        @blk.sync
        def _(e):
            p_plz = snap["pl"] - NT
            o_dst = out.rearrange("(t p) d -> p t d", p=128)
            for t in range(NT):
                wait(e, "pl", p_plz + t + 1)
                inc(e.dma_start(o_dst[:, t, :], xs[:, t, :]), "st", 16)
            for tname in taps:
                tsrc = tap_map[tname]
                tdr = nc.dram_tensor("tap_" + tname, list(tsrc.shape), tsrc.dtype,
                                     kind="ExternalOutput")
                inc(e.dma_start(tdr.ap(), tsrc.ap() if hasattr(tsrc, "ap") else tsrc), "st", 16)
            e.wait_ge(sem("st"), cnt["st"])

    es.close()
    return nc



def _prep_weights(kw):
    asc = np.ascontiguousarray
    f = lambda a: np.asarray(a, np.float32)
    d = {}
    d["l0wq"] = asc(f(kw["L0_attn_Wq"]).T)
    d["l0wk"] = asc(f(kw["L0_attn_Wk"]).T)
    d["l0wv"] = asc(f(kw["L0_attn_Wv"]).T)
    d["l0bq"] = asc(f(kw["L0_attn_bq"]).reshape(8, 128).T)
    d["l0bk"] = asc(f(kw["L0_attn_bk"]).reshape(8, 128).T)
    d["l0bv"] = f(kw["L0_attn_bv"]).reshape(1, 1024)
    d["l0bw"] = asc(f(kw["L0_ffn_base_w"]).T)
    d["l0bb"] = f(kw["L0_ffn_base_b"]).reshape(1, 1024)
    d["l0sm"] = asc(np.concatenate([f(kw["L0_ffn_router_w"]).T, f(kw["L0_ffn_lora_A"])], axis=1))
    d["l0rb"] = np.concatenate([f(kw["L0_ffn_router_b"]).reshape(1, 8),
                            np.zeros((1, 8), np.float32)], axis=1)
    d["l0lb"] = asc(f(kw["L0_ffn_lora_B"]).reshape(64, 1024))
    d["l1wq"] = asc(f(kw["L1_attn_WQ"]))
    d["l1wk"] = asc(f(kw["L1_attn_WK"]))
    d["l1wv"] = asc(f(kw["L1_attn_WV"]))
    d["l1rw"] = asc(f(kw["L1_ffn_w_router"]).T)
    d["l1ce"] = asc(f(kw["L1_ffn_head_centroids"]))
    d["l1tm"] = f(kw["L1_ffn_temperature"]).reshape(1, 1)
    d["l1we"] = asc(f(kw["L1_ffn_input_experts"]))
    d["l1wo"] = asc(f(kw["L1_ffn_output_experts"]))
    return d


def kernel(**inputs):
    x = np.asarray(inputs["L0_attn_x"], np.float32)
    w = _prep_weights(inputs)
    nc = build_program(ALL_PARTS)
    in_maps = []
    for cidx in range(NCORES):
        m = dict(w)
        m["x_in"] = np.ascontiguousarray(x[cidx])
        in_maps.append(m)
    res = run_bass_kernel_spmd(nc, in_maps, core_ids=list(range(NCORES)))
    return np.stack([r["out"] for r in res.results], axis=0)
